# version 07
# baseline (speedup 1.0000x reference)
import math
from contextlib import ExitStack

import numpy as np

import concourse.bass as bass
import concourse.mybir as mybir
from concourse.bass_utils import run_bass_kernel_spmd

F32 = mybir.dt.float32
BF16 = mybir.dt.bfloat16
AF = mybir.ActivationFunctionType
ALU = mybir.AluOpType
AX = mybir.AxisListType

T = 2048
D = 4096
HN = 32
HD = 128
FF = 11008
NT = T // 128
NJ = D // 128
NM = FF // 128
NK2 = 22
ISQ = 1.0 / math.sqrt(HD)
EPS = 1e-5
EXPB = -10.0

ENGINES = ("tensor", "scalar", "vector", "gpsimd", "sync")
N_SUBLAYERS = 3
FFN_STAGES = 5
F2_ALOAD = True
F3_GMM = True
F3_NM = NM
F3_LORA = True
SERIAL = 1
ZSER = 2
F3_SIL = True
F3_ZW = True
F2_UMM = True
F2_EVAC = True


class B:

    def __init__(self, nc, es):
        self.nc = nc
        self.es = es
        self.ops = {e: [] for e in ENGINES}
        self.sems = {}
        self.cnt = {}
        self.base = {}
        self.lastwait = {e: {} for e in ENGINES}
        self.nbar = 0

    def sem(self, name):
        if name not in self.sems:
            self.sems[name] = self.es.enter_context(self.nc.semaphore(name))
            self.cnt[name] = 0
        return self.sems[name]

    def new_phase(self):
        self.base = dict(self.cnt)

    def tok(self, name):
        return self.cnt.get(name, 0) - self.base.get(name, 0)

    def abs_waits(self, waits):
        return [(n, self.base.get(n, 0) + v, "abs") for (n, v) in waits]

    def op(self, eng, body, waits=(), inc=None, inc_n=1):
        wl = []
        for w in waits:
            if len(w) == 3:
                n, v, _ = w
            else:
                n, v = w
                if v is None or v <= 0:
                    continue
                v = self.base.get(n, 0) + v
            if v is None or v <= 0:
                continue
            if v > self.lastwait[eng].get(n, 0):
                self.lastwait[eng][n] = v
                wl.append((self.sem(n), v))
        it = None
        tok = None
        if inc is not None:
            s = self.sem(inc)
            self.cnt[inc] = self.cnt.get(inc, 0) + inc_n
            tok = self.cnt[inc] - self.base.get(inc, 0)
            it = (s, inc_n)

        def fn(E, wl=wl, body=body, it=it):
            for s, v in wl:
                E.wait_ge(s, v)
            inst = body(E)
            if it is not None:
                if inst is None:
                    inst = E.nop()
                inst.then_inc(*it)

        self.ops[eng].append(fn)
        return tok

    def barrier(self):
        self.nbar += 1
        target = 5 * self.nbar
        s = self.sem("bar")
        for e in ENGINES:
            def fn(E, s=s, target=target):
                E.nop().then_inc(s, 1)
                E.wait_ge(s, target)
            self.ops[e].append(fn)
            self.lastwait[e]["bar"] = target

    def emit(self):
        with self.nc.Block() as block:
            @block.tensor
            def _(E):
                for fn in self.ops["tensor"]:
                    fn(E)

            @block.scalar
            def _(E):
                for fn in self.ops["scalar"]:
                    fn(E)

            @block.vector
            def _(E):
                for fn in self.ops["vector"]:
                    fn(E)

            @block.gpsimd
            def _(E):
                for fn in self.ops["gpsimd"]:
                    fn(E)

            @block.sync
            def _(E):
                for fn in self.ops["sync"]:
                    fn(E)


def build_program():
    nc = bass.Bass("TRN2", target_bir_lowering=False, debug=False,
                   detect_race_conditions=False)
    es = ExitStack()
    b = B(nc, es)

    dt = {}
    dt["x"] = nc.dram_tensor("x", [T, D], F32, kind="ExternalInput").ap()
    dt["wq_t"] = nc.dram_tensor("wq_t", [HN, D, HD], F32, kind="ExternalInput").ap()
    dt["wk_t"] = nc.dram_tensor("wk_t", [HN, D, HD], F32, kind="ExternalInput").ap()
    dt["wv_t"] = nc.dram_tensor("wv_t", [HN, D, HD], F32, kind="ExternalInput").ap()
    dt["wo_t"] = nc.dram_tensor("wo_t", [16, D, 256], F32, kind="ExternalInput").ap()
    dt["bq"] = nc.dram_tensor("bq", [128, HN], F32, kind="ExternalInput").ap()
    dt["bk"] = nc.dram_tensor("bk", [128, HN], F32, kind="ExternalInput").ap()
    dt["bv"] = nc.dram_tensor("bv", [128, HN], F32, kind="ExternalInput").ap()
    dt["w1_t"] = nc.dram_tensor("w1_t", [NM, 1024, 128], F32, kind="ExternalInput").ap()
    dt["w3_t"] = nc.dram_tensor("w3_t", [NM, 1024, 128], F32, kind="ExternalInput").ap()
    dt["w2_t"] = nc.dram_tensor("w2_t", [NK2 * 128, D], F32, kind="ExternalInput").ap()
    dt["a1_t"] = nc.dram_tensor("a1_t", [D, 128], F32, kind="ExternalInput").ap()
    dt["a3_t"] = nc.dram_tensor("a3_t", [D, 128], F32, kind="ExternalInput").ap()
    dt["a2_t"] = nc.dram_tensor("a2_t", [88 * 128, 128], F32, kind="ExternalInput").ap()
    dt["b1_t"] = nc.dram_tensor("b1_t", [128, FF], F32, kind="ExternalInput").ap()
    dt["b3_t"] = nc.dram_tensor("b3_t", [128, FF], F32, kind="ExternalInput").ap()
    dt["b2_t"] = nc.dram_tensor("b2_t", [128, D], F32, kind="ExternalInput").ap()
    dt["bo_r"] = nc.dram_tensor("bo_r", [128, D], F32, kind="ExternalInput").ap()

    dt["out"] = nc.dram_tensor("out", [T, D], F32, kind="ExternalOutput").ap()

    dt["LN"] = nc.dram_tensor("LN", [T, D], BF16, kind="Internal").ap()
    dt["X1"] = nc.dram_tensor("X1", [T, D], F32, kind="Internal").ap()
    dt["Z"] = nc.dram_tensor("Z", [NM + 1, 128, T], BF16, kind="Internal").ap()
    dt["QS"] = nc.dram_tensor("QS", [HN, 128, T], BF16, kind="Internal").ap()
    dt["KS"] = nc.dram_tensor("KS", [HN, 128, T], BF16, kind="Internal").ap()
    dt["VS"] = nc.dram_tensor("VS", [HN, 128, T], BF16, kind="Internal").ap()

    sb = lambda name, shape, dtp: es.enter_context(nc.sbuf_tensor(name, shape, dtp))

    it_f = sb("it_f", [128, 128], F32)
    ident = sb("ident", [128, 128], BF16)
    maskT = sb("maskT", [128, 128], F32)
    ones128 = sb("ones128", [128, 128], BF16)
    ident64 = sb("ident64", [128, 128], BF16)
    e0_bf = sb("e0_bf", [128, 128], BF16)
    bqs_sb = sb("bqs_sb", [128, HN], F32)
    bk_sb = sb("bk_sb", [128, HN], F32)
    bv_sb = sb("bv_sb", [128, HN], F32)
    u1_bf = sb("u1_bf", [128, T], BF16)
    u3_bf = sb("u3_bf", [128, T], BF16)
    u2_bf = sb("u2_bf", [128, T], BF16)
    eps_t = sb("eps_t", [128, 1], F32)
    expb_t = sb("expb_t", [128, 1], F32)
    zzero = sb("zzero", [128, T], BF16)

    b.op("gpsimd", lambda g: g.iota(it_f[:], [[1, 128]], channel_multiplier=-1,
                                    allow_small_or_imprecise_dtypes=True), inc="p0g")
    t0 = b.tok("p0g")

    def mk_consts(v):
        v.tensor_single_scalar(out=maskT[:], in_=it_f[:], scalar=0.0, op=ALU.is_lt)
        v.tensor_scalar_mul(out=maskT[:], in0=maskT[:], scalar1=-1e30)
        v.tensor_single_scalar(out=it_f[:], in_=it_f[:], scalar=0.0, op=ALU.is_equal)
        v.tensor_copy(out=ident[:], in_=it_f[:])
        v.memset(ident64[:], 0.0)
        v.tensor_copy(out=ident64[0:64, :], in_=ident[0:64, :])
        v.memset(ones128[:], 1.0)
        v.memset(eps_t[:], EPS)
        v.memset(expb_t[:], EXPB)
        v.memset(u1_bf[:], 0.0)
        v.memset(u3_bf[:], 0.0)
        v.memset(zzero[:], 0.0)
        return v.memset(u2_bf[:], 0.0)
    b.op("vector", mk_consts, waits=[("p0g", t0)], inc="p0v")
    b.op("gpsimd", lambda g: g.iota(it_f[:], [[0, 128]], channel_multiplier=1,
                                    allow_small_or_imprecise_dtypes=True),
         waits=[("p0v", 1)], inc="p0g")
    t0b = b.tok("p0g")
    def mk_e0(v):
        v.tensor_single_scalar(out=it_f[:], in_=it_f[:], scalar=0.0, op=ALU.is_equal)
        return v.tensor_copy(out=e0_bf[:], in_=it_f[:])
    b.op("vector", mk_e0, waits=[("p0g", t0b)], inc="p0v")

    b.op("sync", lambda s: s.dma_start(out=bqs_sb[:], in_=dt["bq"]), inc="p0e", inc_n=16)
    b.op("sync", lambda s: s.dma_start(out=bk_sb[:], in_=dt["bk"]), inc="p0e", inc_n=16)
    b.op("sync", lambda s: s.dma_start(out=bv_sb[:], in_=dt["bv"]), inc="p0e", inc_n=16)
    t1 = b.tok("p0e")
    b.op("vector", lambda v: v.tensor_scalar_mul(out=bqs_sb[:], in0=bqs_sb[:], scalar1=ISQ),
         waits=[("p0e", t1)], inc="p0v")
    b.barrier()

    def ln_pass(src, tag, ph, ready, prefill_dst=None):
        s_x, s_st, s_rs, s_ln = "Lx", "Lst", "Lrs", "Lln"
        s_lnw, s_pref, s_sq2 = "Llnw", "Lpf", "Lsq2"
        psb = lambda name, shape, dtp: ph.enter_context(
            nc.sbuf_tensor(tag + name, shape, dtp))
        xt = [psb(f"xt{i}", [128, D], F32) for i in range(2)]
        sq = [psb(f"sq{i}", [128, D], F32) for i in range(2)]
        lnt = [psb(f"lnt{i}", [128, D], BF16) for i in range(2)]
        stm = [psb(f"stm{i}", [128, 1], F32) for i in range(2)]
        stv = [psb(f"stv{i}", [128, 1], F32) for i in range(2)]
        str_ = [psb(f"str{i}", [128, 1], F32) for i in range(2)]
        for t in range(NT):
            p = t % 2
            w = [(s_ln, t - 1)]
            if prefill_dst is not None:
                w.append((s_pref + str(p), 16 * (t // 2)))
            b.op("sync", lambda s, t=t, p=p: s.dma_start(
                out=xt[p][:], in_=src[t * 128:(t + 1) * 128, :]),
                waits=list(ready) + w, inc=s_x + str(p), inc_n=16)
            xl = (s_x + str(p), b.tok(s_x + str(p)))
            if prefill_dst is not None:
                b.op("gpsimd", lambda g, t=t, p=p: g.dma_start(
                    out=prefill_dst[t * 128:(t + 1) * 128, :], in_=xt[p][:]),
                    waits=[xl], inc=s_pref + str(p), inc_n=16)

            b.op("scalar", lambda s, p=p: s.activation(
                out=sq[p][:], in_=xt[p][:], func=AF.Square,
                accum_out=stv[p][:]),
                waits=[xl, (s_st, 3 * (t - 1))], inc=s_sq2)
            sq2t = b.tok(s_sq2)
            b.op("vector", lambda v, p=p: v.reduce_sum(
                out=stm[p][:], in_=xt[p][:], axis=AX.X),
                waits=[xl], inc=s_st)
            def stats2(v, p=p):
                v.tensor_scalar_mul(out=stm[p][:], in0=stm[p][:], scalar1=1.0 / D)
                return v.tensor_scalar_mul(out=stv[p][:], in0=stv[p][:],
                                           scalar1=1.0 / D)
            b.op("vector", stats2, waits=[(s_st, b.tok(s_st)), (s_sq2, sq2t)],
                 inc=s_st)
            b.op("vector", lambda v, p=p: v.scalar_tensor_tensor(
                out=stv[p][:], in0=stm[p][:], scalar=stm[p][:], in1=stv[p][:],
                op0=ALU.mult, op1=ALU.subtract),
                waits=[(s_st, b.tok(s_st))], inc=s_st)

            b.op("scalar", lambda s, p=p: s.activation(
                out=str_[p][:], in_=stv[p][:], func=AF.Ln,
                scale=-1.0, bias=eps_t[:]),
                waits=[(s_st, b.tok(s_st))], inc=s_rs)
            b.op("scalar", lambda s, p=p: s.activation(
                out=str_[p][:], in_=str_[p][:], func=AF.Exp, scale=-0.5),
                waits=[(s_rs, b.tok(s_rs))], inc=s_rs)
            b.op("vector", lambda v, p=p: v.tensor_scalar(
                out=lnt[p][:], in0=xt[p][:], scalar1=stm[p][:], scalar2=str_[p][:],
                op0=ALU.subtract, op1=ALU.mult),
                waits=[(s_rs, b.tok(s_rs)), (s_lnw + str(p), 16 * (t // 2))],
                inc=s_ln)
            b.op("sync", lambda s, t=t, p=p: s.dma_start(
                out=dt["LN"][t * 128:(t + 1) * 128, :], in_=lnt[p][:]),
                waits=[(s_ln, b.tok(s_ln))], inc=s_lnw + str(p), inc_n=16)
        lnw_w = [(s_lnw + "0", b.tok(s_lnw + "0")), (s_lnw + "1", b.tok(s_lnw + "1"))]
        pref_w = ([(s_pref + "0", b.tok(s_pref + "0")),
                   (s_pref + "1", b.tok(s_pref + "1"))]
                  if prefill_dst is not None else [])
        return lnw_w, pref_w

    def ffn_phase(src, dst, tag, ready):
        b.new_phase()
        s_lnr = "Flnr"
        s_u = "Fu"
        s_ue = "Fue"
        s_xs = "Fxs"
        s_xc = "Fxc"
        s_w = "Fw"
        s_pw = "Fpw"
        s_g = "Fg"
        s_si = "Fsi"
        s_z = "Fz"
        s_zw = "Fzw"
        s_zr = "Fzr"
        s_zs = "Fzs"
        s_zc = "Fzc"
        s_u2 = "Fu2"
        s_u2e = "Fu2e"
        s_w2 = "Fw2"
        s_y = "Fy"
        s_ye = "Fye"
        s_acc = "Facc"

        with ExitStack() as ph:
            psb = lambda name, shape, dtp: ph.enter_context(
                nc.sbuf_tensor(tag + name, shape, dtp))

            with ExitStack() as s1:
                lnw_w, pref_w = ln_pass(src, tag + "a", s1, ready, prefill_dst=dst)
            if FFN_STAGES <= 1:
                b.op("sync", lambda s: None, waits=pref_w + lnw_w)
                return b.abs_waits(pref_w)

            sx = ExitStack()
            xs_bf = [sx.enter_context(nc.sbuf_tensor(tag + f"xsb{i}", [128, T], BF16))
                     for i in range(8)]

            with ExitStack() as s2:
                p2 = lambda name, shape, dtp: s2.enter_context(
                    nc.sbuf_tensor(tag + name, shape, dtp))
                a1t_sb = p2("a1t", [128, HN * 128], BF16)
                a3t_sb = p2("a3t", [128, HN * 128], BF16)
                lnf = [p2(f"lnf{i}", [128, T], BF16) for i in range(3)]
                xs_acc = [p2(f"xsa{i}", [128, T], F32) for i in range(8)]
                if F2_ALOAD:
                    b.op("gpsimd", lambda g: g.dma_start(
                        out=a1t_sb[:].rearrange("p (j f) -> p j f", j=HN),
                        in_=dt["a1_t"].rearrange("(j p) f -> p j f", p=128)),
                        waits=lnw_w + pref_w,
                        inc="Faw", inc_n=16)
                    b.op("gpsimd", lambda g: g.dma_start(
                        out=a3t_sb[:].rearrange("p (j f) -> p j f", j=HN),
                        in_=dt["a3_t"].rearrange("(j p) f -> p j f", p=128)),
                        inc="Faw", inc_n=16)
                aw_all = b.tok("Faw")
                with ExitStack() as pst:
                    pu1 = [pst.enter_context(nc.psum_tensor(tag + f"pu1_{i}",
                           [128, 512], F32)) for i in range(4)]
                    pu3 = [pst.enter_context(nc.psum_tensor(tag + f"pu3_{i}",
                           [128, 512], F32)) for i in range(4)]
                    for j in range(NJ):
                        p = j % 3
                        b.op("sync", lambda s, j=j, p=p: s.dma_start_transpose(
                            out=lnf[p][:], in_=dt["LN"][:, j * 128:(j + 1) * 128]),
                            waits=lnw_w + pref_w + [(s_xs, j - 2)] +
                                  ([(s_u, 8 * (j - 2))] if F2_UMM else []),
                            inc=s_lnr + str(p), inc_n=16)
                        lr = (s_lnr + str(p), b.tok(s_lnr + str(p)))

                        def umm(t_, j=j, p=p):
                            last = None
                            for fb in range(4):
                                t_.matmul(out=pu1[fb][:],
                                          lhsT=a1t_sb[:, j * 128:(j + 1) * 128],
                                          rhs=lnf[p][:, fb * 512:(fb + 1) * 512],
                                          start=(j == 0), stop=(j == NJ - 1))
                            for fb in range(4):
                                last = t_.matmul(out=pu3[fb][:],
                                          lhsT=a3t_sb[:, j * 128:(j + 1) * 128],
                                          rhs=lnf[p][:, fb * 512:(fb + 1) * 512],
                                          start=(j == 0), stop=(j == NJ - 1))
                            return last
                        if F2_UMM:
                            b.op("tensor", umm,
                                 waits=[lr, ("Faw", aw_all)],
                                 inc=s_u, inc_n=8)
                        jp = j % 8
                        if j < 8:
                            b.op("vector", lambda v, jp=jp, p=p: v.tensor_copy(
                                out=xs_acc[jp][:], in_=lnf[p][:]),
                                waits=[lr], inc=s_xs)
                        else:
                            b.op("vector", lambda v, jp=jp, p=p: v.tensor_tensor(
                                out=xs_acc[jp][:], in0=xs_acc[jp][:], in1=lnf[p][:],
                                op=ALU.add),
                                waits=[lr], inc=s_xs)
                    xs_all = b.tok(s_xs)
                    for jp in range(8):
                        b.op("vector", lambda v, jp=jp: v.tensor_copy(
                            out=xs_bf[jp][:], in_=xs_acc[jp][:]),
                            waits=[(s_xs, xs_all)], inc=s_xc)
                    u_all = b.tok(s_u)
                    if F2_EVAC:
                        def uev(s):
                            last = None
                            for fb in range(4):
                                s.activation(out=u1_bf[:, fb * 512:(fb + 1) * 512],
                                             in_=pu1[fb][:], func=AF.Identity,
                                             scale=0.5)
                                last = s.activation(
                                    out=u3_bf[:, fb * 512:(fb + 1) * 512],
                                    in_=pu3[fb][:], func=AF.Identity, scale=0.5)
                            return last
                        b.op("scalar", uev, waits=[(s_u, u_all)], inc=s_ue)
                        b.op("scalar", lambda s: None,
                             waits=[(s_ue, b.tok(s_ue))], inc=s_ue)
            ue_all = b.tok(s_ue)
            xc_all = b.tok(s_xc)
            if FFN_STAGES <= 2:
                b.op("sync", lambda s: None,
                     waits=[(s_xc, xc_all), (s_ue, ue_all)] + pref_w)
                sx.close()
                return b.abs_waits(pref_w)

            with ExitStack() as s3:
                p3 = lambda name, shape, dtp: s3.enter_context(
                    nc.sbuf_tensor(tag + name, shape, dtp))
                w1sb = [p3(f"w1s{i}", [128, 1024], BF16) for i in range(2)]
                w3sb = [p3(f"w3s{i}", [128, 1024], BF16) for i in range(2)]
                b1sb = p3("b1s", [128, FF], BF16)
                b3sb = p3("b3s", [128, FF], BF16)
                sil = [p3(f"sil{i}", [128, 512], F32) for i in range(4)]
                zst = [p3(f"zst{i}", [128, T], BF16) for i in range(2)]
                gate2 = [(s_xc, xc_all), (s_ue, ue_all)]
                b.op("gpsimd", lambda g: g.dma_start(out=b1sb[:], in_=dt["b1_t"]),
                     waits=gate2, inc=s_w + "b", inc_n=16)
                b.op("gpsimd", lambda g: g.dma_start(out=b3sb[:], in_=dt["b3_t"]),
                     inc=s_w + "b", inc_n=16)
                wb = (s_w + "b", b.tok(s_w + "b"))
                with ExitStack() as pgs:
                    pgt = [pgs.enter_context(nc.psum_tensor(tag + f"pg{i}",
                           [128, 512], F32)) for i in range(8)]
                    pg = [(pgt[0], pgt[1]), (pgt[2], pgt[3]),
                          (pgt[4], pgt[5]), (pgt[6], pgt[7])]
                    for m in range(F3_NM):
                        p = m % 2
                        b.op("gpsimd", lambda g, m=m, p=p: g.dma_start(
                            out=w1sb[p][:].rearrange("p (k f) -> p k f", k=8),
                            in_=dt["w1_t"][m].rearrange("(k p) f -> p k f", p=128)),
                            waits=[(s_pw, m - 1)], inc=s_w + str(p), inc_n=16)
                        b.op("gpsimd", lambda g, m=m, p=p: g.dma_start(
                            out=w3sb[p][:].rearrange("p (k f) -> p k f", k=8),
                            in_=dt["w3_t"][m].rearrange("(k p) f -> p k f", p=128)),
                            inc=s_w + str(p), inc_n=16)
                        wt = (s_w + str(p), b.tok(s_w + str(p)))
                        for tq in range(4):
                            nz = 4 * m + tq
                            pp = nz % 4
                            g1p, g3p = pg[pp]
                            sl = slice(tq * 512, (tq + 1) * 512)
                            if not F3_GMM:
                                continue

                            def gmm(t_, m=m, p=p, sl=sl, g1p=g1p, g3p=g3p):
                                for k in range(8):
                                    t_.matmul(out=g1p[:],
                                              lhsT=w1sb[p][:, k * 128:(k + 1) * 128],
                                              rhs=xs_bf[k][:, sl],
                                              start=(k == 0), stop=not (F3_LORA or k < 7))
                                if F3_LORA:
                                    t_.matmul(out=g1p[:],
                                              lhsT=b1sb[:, m * 128:(m + 1) * 128],
                                              rhs=u1_bf[:, sl], start=False, stop=True)
                                for k in range(8):
                                    last = t_.matmul(out=g3p[:],
                                              lhsT=w3sb[p][:, k * 128:(k + 1) * 128],
                                              rhs=xs_bf[k][:, sl],
                                              start=(k == 0), stop=not (F3_LORA or k < 7))
                                if F3_LORA:
                                    last = t_.matmul(out=g3p[:],
                                          lhsT=b3sb[:, m * 128:(m + 1) * 128],
                                          rhs=u3_bf[:, sl], start=False, stop=True)
                                return last
                            b.op("tensor", gmm,
                                 waits=[wt, wb, (s_xc, xc_all), (s_ue, ue_all)] +
                                       ([(s_z, nz - 3)] if F3_SIL else
                                        [(s_g, nz - 1)]),
                                 inc=s_g)
                            gt = b.tok(s_g)
                            if not F3_SIL:
                                continue
                            b.op("scalar", lambda s, pp=pp, g1p=g1p: s.activation(
                                out=sil[pp][:], in_=g1p[:], func=AF.Silu),
                                waits=[(s_g, gt), (s_z, nz - 3)], inc=s_si)
                            sit = b.tok(s_si)
                            b.op("vector", lambda v, pp=pp, p=p, sl=sl, g3p=g3p:
                                 v.tensor_tensor(out=zst[p][:, sl], in0=sil[pp][:],
                                                 in1=g3p[:], op=ALU.mult),
                                 waits=[(s_si, sit)] +
                                       ([(s_zw + str(p), 16 * (m // 2))]
                                        if F3_ZW else []),
                                 inc=s_z)
                        zt = b.tok(s_z)
                        b.op("tensor", lambda t_: None,
                             waits=[(s_g, b.tok(s_g))], inc=s_pw)
                        if F3_ZW and F3_SIL:
                            b.op("sync", lambda s, m=m, p=p: s.dma_start(
                                out=dt["Z"][m], in_=zst[p][:]),
                                waits=[(s_z, zt)], inc=s_zw + str(p), inc_n=16)
                zw_w = [(s_zw + "0", b.tok(s_zw + "0")),
                        (s_zw + "1", b.tok(s_zw + "1"))]
            sx.close()
            if FFN_STAGES <= 3:
                b.op("sync", lambda s: None,
                     waits=zw_w + [(s_z, b.tok(s_z)), (s_g, b.tok(s_g))] + pref_w)
                return b.abs_waits(pref_w)

            zs_bf = [psb(f"zsb{i}", [128, T], BF16) for i in range(NK2)]
            zflat = dt["Z"].rearrange("m p t -> (m p) t")
            b.op("sync", lambda s: s.dma_start(out=dt["Z"][NM], in_=zzero[:]),
                 waits=[("p0v", 3, "abs")], inc="Fzp", inc_n=16)
            fzp = ("Fzp", b.tok("Fzp"))
            with ExitStack() as s45:
                p45 = lambda name, shape, dtp: s45.enter_context(
                    nc.sbuf_tensor(tag + name, shape, dtp))
                a2t_sb = p45("a2t", [128, 88 * 128], BF16)
                zld = [p45(f"zld{i}", [128, T], BF16) for i in range(5)]
                w2sb = [p45(f"w2s{i}", [128, NK2 * 512], BF16) for i in range(2)]
                b2sb = p45("b2s", [128, D], BF16)
                ysb = [p45(f"ysb{i}", [128, 512], F32) for i in range(2)]
                b.op("gpsimd", lambda g: g.dma_start(
                    out=a2t_sb[:].rearrange("p (m f) -> p m f", m=88),
                    in_=dt["a2_t"].rearrange("(m p) f -> p m f", p=128)),
                    waits=zw_w + [fzp], inc="Fa2w", inc_n=16)
                a2w_all = b.tok("Fa2w")
                b.op("gpsimd", lambda g: g.dma_start(out=b2sb[:], in_=dt["b2_t"]),
                     waits=zw_w + [fzp], inc=s_w2 + "b", inc_n=16)
                w2b = (s_w2 + "b", b.tok(s_w2 + "b"))
                with (
                    nc.psum_tensor(tag + "pu2q0", [128, 512], F32) as pu2q0,
                    nc.psum_tensor(tag + "pu2q1", [128, 512], F32) as pu2q1,
                    nc.psum_tensor(tag + "pu2q2", [128, 512], F32) as pu2q2,
                    nc.psum_tensor(tag + "pu2q3", [128, 512], F32) as pu2q3,
                    nc.psum_tensor(tag + "zp0", [128, 512], F32) as zp0,
                    nc.psum_tensor(tag + "zp1", [128, 512], F32) as zp1,
                    nc.psum_tensor(tag + "py0", [128, 512], F32) as py0,
                    nc.psum_tensor(tag + "py1", [128, 512], F32) as py1,
                ):
                    pu2q = [pu2q0, pu2q1, pu2q2, pu2q3]
                    zp = [zp0, zp1]
                    py = [py0, py1]
                    zrs = []
                    for kk in range(NK2):
                        for c in range(4):
                            idx = kk * 4 + c
                            p = idx % 5
                            r0 = 2752 * c + 128 * kk
                            m0 = r0 // 128
                            m1 = (r0 + 127) // 128
                            lw = []
                            for mm_ in {m0, m1}:
                                if mm_ >= NM:
                                    lw.append(fzp)
                                else:
                                    lw.append((s_zw + str(mm_ % 2),
                                               16 * (mm_ // 2 + 1)))
                            if idx >= 5:
                                lw.append((s_u2, idx - 4))
                                lw.append(("Fzm", (idx - 5) // 4 + 1))
                            b.op("sync", lambda s, p=p, r0=r0:
                                 s.dma_start(out=zld[p][:],
                                             in_=zflat[r0:r0 + 128, :]),
                                 waits=lw, inc=s_zr + str(p), inc_n=16)
                            zr = (s_zr + str(p), b.tok(s_zr + str(p)))
                            zrs.append(zr)
                            blk = c * NK2 + kk
                            rows = 64 if kk == NK2 - 1 else 128

                            def u2mm(t_, p=p, blk=blk, rows=rows, idx=idx):
                                last = None
                                for q in range(4):
                                    last = t_.matmul(
                                        out=pu2q[q][:],
                                        lhsT=a2t_sb[0:rows,
                                                    blk * 128:(blk + 1) * 128],
                                        rhs=zld[p][0:rows,
                                                   q * 512:(q + 1) * 512],
                                        start=(idx == 0), stop=(idx == 87))
                                return last
                            b.op("tensor", u2mm,
                                 waits=[zr, ("Fa2w", a2w_all)], inc=s_u2)
                        idn = ident64 if kk == NK2 - 1 else ident
                        for fb in range(4):
                            ze = kk * 4 + fb
                            pz = ze % 2

                            def zsmm(t_, kk=kk, fb=fb, pz=pz, idn=idn):
                                last = None
                                for c in range(4):
                                    pbuf = (kk * 4 + c) % 5
                                    last = t_.matmul(
                                        out=zp[pz][:],
                                        lhsT=idn[:],
                                        rhs=zld[pbuf][:, fb * 512:(fb + 1) * 512],
                                        start=(c == 0), stop=(c == 3))
                                return last
                            b.op("tensor", zsmm,
                                 waits=zrs[kk * 4:kk * 4 + 4] +
                                       [(s_zc, ze - 1)],
                                 inc="Fzm" if fb == 3 else "Fzmq")
                            b.op("scalar", lambda s, kk=kk, fb=fb, pz=pz:
                                 s.activation(
                                     out=zs_bf[kk][:, fb * 512:(fb + 1) * 512],
                                     in_=zp[pz][:], func=AF.Identity),
                                 waits=[("Fzm" if fb == 3 else "Fzmq",
                                         b.tok("Fzm" if fb == 3 else "Fzmq"))],
                                 inc=s_zc)
                    u2t = b.tok(s_u2)
                    def u2ev(s):
                        last = None
                        for q in range(4):
                            last = s.activation(
                                out=u2_bf[:, q * 512:(q + 1) * 512],
                                in_=pu2q[q][:], func=AF.Identity, scale=0.5)
                        return last
                    b.op("scalar", u2ev, waits=[(s_u2, u2t)], inc=s_u2e)
                    zc_all = b.tok(s_zc)
                    u2e_all = b.tok(s_u2e)
                    if FFN_STAGES <= 4:
                        b.op("sync", lambda s: None,
                             waits=[(s_zc, zc_all), (s_u2e, u2e_all)] + pref_w)
                        return b.abs_waits(pref_w)

                    for fb in range(8):
                        p = fb % 2
                        fsl = slice(fb * 512, (fb + 1) * 512)
                        b.op("gpsimd", lambda g, p=p, fsl=fsl: g.dma_start(
                            out=w2sb[p][:].rearrange("p (k f) -> p k f", k=NK2),
                            in_=dt["w2_t"][:, fsl].rearrange("(k p) f -> p k f", p=128)),
                            waits=zw_w + [fzp, (s_y, NT * (fb - 1))],
                            inc=s_w2 + str(p), inc_n=16)
                        w2t_ = (s_w2 + str(p), b.tok(s_w2 + str(p)))
                        for t in range(NT):
                            nzy = fb * NT + t
                            pp = t % 2
                            tsl = slice(t * 128, (t + 1) * 128)

                            def ymm(t_, p=p, pp=pp, tsl=tsl, fb=fb):
                                for k in range(NK2):
                                    t_.matmul(
                                        out=py[pp][:],
                                        lhsT=zs_bf[k][:, tsl],
                                        rhs=w2sb[p][:, k * 512:(k + 1) * 512],
                                        start=(k == 0), stop=False)
                                return t_.matmul(
                                    out=py[pp][:],
                                    lhsT=u2_bf[:, tsl],
                                    rhs=b2sb[:, fb * 512:(fb + 1) * 512],
                                    start=False, stop=True)
                            b.op("tensor", ymm,
                                 waits=[w2t_, w2b, (s_zc, zc_all),
                                        (s_u2e, u2e_all), (s_ye, nzy - 1)],
                                 inc=s_y)
                            yt = b.tok(s_y)
                            b.op("scalar", lambda s, pp=pp: s.activation(
                                out=ysb[pp][:], in_=py[pp][:], func=AF.Identity,
                                scale=0.5),
                                waits=[(s_y, yt),
                                       (s_acc + str(pp), 16 * ((nzy) // 2))],
                                inc=s_ye)
                            yet = b.tok(s_ye)
                            b.op("gpsimd", lambda g, tsl=tsl, fsl=fsl, pp=pp:
                                 g.dma_start(out=dst[tsl, fsl], in_=ysb[pp][:],
                                             accum_op=ALU.add),
                                 waits=[(s_ye, yet)] + pref_w,
                                 inc=s_acc + str(pp), inc_n=16)
        return b.abs_waits([(s_acc + "0", b.tok(s_acc + "0")),
                            (s_acc + "1", b.tok(s_acc + "1"))])

    def attn_phase(src, dst, tag, ready):
        b.new_phase()
        s_lnr = "Blnr"
        s_w = "Fw"
        s_p = "Bp"
        s_pe = "Bpe"
        s_qw = "Bqw"
        s_hl = "Bhl"
        s_vt = "Bvt"
        s_ve = "Bve"
        s_stm = "Bsm"
        s_msk = "Bmk"
        s_ex = "Bex"
        s_av = "Bav"
        s_oe = "Boe"
        s_rc = "Brc"
        s_cb = "Bcb"
        s_of = "Bof"
        s_wo = "Bwo"
        s_yo = "Fy"
        s_ye = "Fye"
        s_acc = "Facc"

        with ExitStack() as ph:
            psb = lambda name, shape, dtp: ph.enter_context(
                nc.sbuf_tensor(tag + name, shape, dtp))

            with ExitStack() as a1s:
                lnw_w, _ = ln_pass(src, tag + "a", a1s, ready)

            with ExitStack() as a2s:
                p2 = lambda name, shape, dtp: a2s.enter_context(
                    nc.sbuf_tensor(tag + name, shape, dtp))
                ln2 = [p2(f"ln2_{j}", [128, T], BF16) for j in range(NJ)]
                wsb = [p2(f"wsb{i}", [128, D], BF16) for i in range(3)]
                qkv_sb = [p2(f"qkv{i}", [128, T], BF16) for i in range(2)]
                for j in range(NJ):
                    b.op("sync", lambda s, j=j: s.dma_start_transpose(
                        out=ln2[j][:], in_=dt["LN"][:, j * 128:(j + 1) * 128]),
                        waits=lnw_w, inc=s_lnr, inc_n=16)
                lnr_all = b.tok(s_lnr)
                projs = [(dt["wq_t"], dt["QS"], bqs_sb, ISQ),
                         (dt["wk_t"], dt["KS"], bk_sb, 1.0),
                         (dt["wv_t"], dt["VS"], bv_sb, 1.0)]
                with ExitStack() as pst:
                    pq = [[pst.enter_context(nc.psum_tensor(tag + f"pq{i}_{fb}",
                           [128, 512], F32)) for fb in range(4)] for i in range(2)]
                    nproj = 0
                    for h in range(HN):
                        for (wsrc, dsts, bias, scl) in projs:
                            wp = nproj % 3
                            pp = nproj % 2
                            b.op("gpsimd", lambda g, h=h, wsrc=wsrc, wp=wp:
                                 g.dma_start(
                                     out=wsb[wp][:].rearrange("p (j f) -> p j f", j=NJ),
                                     in_=wsrc[h].rearrange("(j p) f -> p j f", p=128)),
                                 waits=lnw_w + [(s_p, nproj - 2)],
                                 inc=s_w + str(wp), inc_n=16)
                            wt = (s_w + str(wp), b.tok(s_w + str(wp)))

                            def pmm(t_, wp=wp, pp=pp):
                                last = None
                                for j in range(NJ):
                                    for fb in range(4):
                                        last = t_.matmul(
                                            out=pq[pp][fb][:],
                                            lhsT=wsb[wp][:, j * 128:(j + 1) * 128],
                                            rhs=ln2[j][:, fb * 512:(fb + 1) * 512],
                                            start=(j == 0), stop=(j == NJ - 1))
                                return last
                            b.op("tensor", pmm,
                                 waits=[wt, (s_lnr, lnr_all),
                                        (s_pe, nproj - 1)],
                                 inc=s_p)
                            pt = b.tok(s_p)
                            def qev(s, pp=pp, bias=bias, scl=scl, h=h):
                                last = None
                                for fb in range(4):
                                    last = s.activation(
                                        out=qkv_sb[pp][:, fb * 512:(fb + 1) * 512],
                                        in_=pq[pp][fb][:], func=AF.Identity,
                                        bias=bias[:, h:h + 1], scale=scl)
                                return last
                            b.op("scalar", qev,
                                 waits=[(s_p, pt),
                                        (s_qw + str(pp), 16 * (nproj // 2))],
                                 inc=s_pe)
                            pet = b.tok(s_pe)
                            b.op("sync", lambda s, dsts=dsts, h=h, pp=pp:
                                 s.dma_start(out=dsts[h], in_=qkv_sb[pp][:]),
                                 waits=[(s_pe, pet)], inc=s_qw + str(pp), inc_n=16)
                            nproj += 1
            qw_w = [(s_qw + "0", b.tok(s_qw + "0")), (s_qw + "1", b.tok(s_qw + "1"))]

            o_all = [psb(f"oall{k}", [128, T], BF16) for k in range(HN)]
            with ExitStack() as a3s:
                p3 = lambda name, shape, dtp: a3s.enter_context(
                    nc.sbuf_tensor(tag + name, shape, dtp))
                qh = [p3(f"qh{i}", [128, T], BF16) for i in range(2)]
                kh = [p3(f"kh{i}", [128, T], BF16) for i in range(2)]
                vh = [p3(f"vh{i}", [128, T], BF16) for i in range(2)]
                vtm = [p3(f"vtm{i}", [128, T], BF16) for i in range(2)]
                att = [p3(f"att{i}", [128, 512], BF16) for i in range(4)]
                lntmp = p3("lntmp", [128, 1024], F32)
                rC = p3("rC", [128, 1024], F32)
                with (
                    nc.psum_tensor(tag + "ps0", [128, 512], F32) as ps0,
                    nc.psum_tensor(tag + "ps1", [128, 512], F32) as ps1,
                    nc.psum_tensor(tag + "ps2", [128, 512], F32) as ps2,
                    nc.psum_tensor(tag + "ps3", [128, 512], F32) as ps3,
                    nc.psum_tensor(tag + "pO0", [128, 512], F32) as pO0,
                    nc.psum_tensor(tag + "pO1", [128, 512], F32) as pO1,
                    nc.psum_tensor(tag + "pl0", [128, 512], F32) as pl0,
                    nc.psum_tensor(tag + "pl1", [128, 512], F32) as pl1,
                ):
                    pO = [pO0, pO1]
                    pl = [pl0, pl1]
                    ps = [ps0, ps1, ps2, ps3]
                    nblk = 0
                    for h in range(HN):
                        p = h % 2
                        b.op("sync", lambda s, h=h, p=p: s.dma_start(
                            out=qh[p][:], in_=dt["QS"][h]),
                            waits=qw_w +
                                  ([(s_of, 2 * (h - 1))] if h >= 2 else []),
                            inc=s_hl + str(p), inc_n=16)
                        b.op("sync", lambda s, h=h, p=p: s.dma_start(
                            out=kh[p][:], in_=dt["KS"][h]), inc=s_hl + str(p), inc_n=16)
                        b.op("sync", lambda s, h=h, p=p: s.dma_start(
                            out=vh[p][:], in_=dt["VS"][h]), inc=s_hl + str(p), inc_n=16)
                        hl = (s_hl + str(p), b.tok(s_hl + str(p)))
                        exp_prev = b.tok(s_ex)
                        for tt in range(NT):
                            pv = tt % 4
                            pst = ps[pv].bitcast(BF16)[:, 0:128]
                            b.op("tensor", lambda t_, p=p, tt=tt, pst=pst:
                                 t_.transpose(out=pst,
                                              in_=vh[p][:, tt * 128:(tt + 1) * 128],
                                              identity=ident[:]),
                                 waits=[hl, (s_ex, exp_prev),
                                        (s_ve, h * NT + tt - 3)],
                                 inc=s_vt)
                            vt = b.tok(s_vt)
                            b.op("scalar", lambda s, p=p, tt=tt, pst=pst:
                                 s.activation(out=vtm[p][:, tt * 128:(tt + 1) * 128],
                                              in_=pst, func=AF.Identity),
                                 waits=[(s_vt, vt)], inc=s_ve)
                        ve_all = b.tok(s_ve)
                        for half in range(2):
                            q0, q1 = half * 1024, (half + 1) * 1024
                            oe_gate = b.tok(s_oe)
                            of_gate = b.tok(s_of)
                            blocks = []
                            for j in range(16):
                                c0 = max(128 * j, q0)
                                if c0 >= q1:
                                    continue
                                cb = c0
                                while cb < q1:
                                    ce = min(((cb // 512) + 1) * 512, q1)
                                    blocks.append((j, cb, ce))
                                    cb = ce
                            SKEW = 3
                            pend = []
                            seq = list(blocks) + [None] * SKEW
                            for blk_i, blk in enumerate(seq):
                                if blk is not None:
                                    (j, cb, ce) = blk
                                    pp2 = nblk % 4
                                    diag = (cb <= 128 * j < ce)
                                    b.op("tensor", lambda t_, j=j, cb=cb, ce=ce,
                                         pp2=pp2, p=p: t_.matmul(
                                             out=ps[pp2][:, 0:ce - cb],
                                             lhsT=kh[p][:, j * 128:(j + 1) * 128],
                                             rhs=qh[p][:, cb:ce],
                                             start=True, stop=True),
                                         waits=[hl, (s_ve, ve_all),
                                                (s_ex, nblk - 3)],
                                         inc=s_stm)
                                    smt = b.tok(s_stm)
                                    if diag:
                                        off = 128 * j - cb
                                        b.op("vector", lambda v, pp2=pp2, off=off:
                                             v.tensor_tensor(
                                                 out=ps[pp2][:, off:off + 128],
                                                 in0=ps[pp2][:, off:off + 128],
                                                 in1=maskT[:], op=ALU.add),
                                             waits=[(s_stm, smt)], inc=s_msk)
                                        mw = [(s_msk, b.tok(s_msk))]
                                    else:
                                        mw = [(s_stm, smt)]
                                    pa = nblk % 4
                                    b.op("scalar", lambda s, pp2=pp2, pa=pa, cb=cb,
                                         ce=ce: s.activation(
                                             out=att[pa][:, 0:ce - cb],
                                             in_=ps[pp2][:, 0:ce - cb],
                                             func=AF.Exp, bias=expb_t[:]),
                                         waits=mw + [(s_av, nblk - 3)], inc=s_ex)
                                    pend.append((j, cb, ce, pa, b.tok(s_ex)))
                                    nblk += 1
                                if len(pend) > (SKEW if blk is not None else 0):
                                    (j, cb, ce, pa, ext) = pend.pop(0)

                                    def avmm(t_, j=j, cb=cb, ce=ce, pa=pa, p=p,
                                             q0=q0):
                                        st = (j == 0)
                                        sp = (128 * (j + 1) >= ce)
                                        r = (cb - q0) // 512
                                        o0 = cb - q0 - 512 * r
                                        o1 = ce - q0 - 512 * r
                                        t_.matmul(out=pO[r][:, o0:o1],
                                                  lhsT=vtm[p][:, j * 128:(j + 1) * 128],
                                                  rhs=att[pa][:, 0:ce - cb],
                                                  start=st, stop=sp)
                                        return t_.matmul(out=pl[r][:, o0:o1],
                                                         lhsT=ones128[:],
                                                         rhs=att[pa][:, 0:ce - cb],
                                                         start=st, stop=sp)
                                    b.op("tensor", avmm,
                                         waits=[(s_ex, ext), (s_oe, oe_gate),
                                                (s_of, of_gate)],
                                         inc=s_av)
                            avt = b.tok(s_av)
                            def lnev(s):
                                s.activation(out=lntmp[:, 0:512], in_=pl[0][:],
                                             func=AF.Ln)
                                return s.activation(out=lntmp[:, 512:1024],
                                                    in_=pl[1][:], func=AF.Ln)
                            b.op("scalar", lnev, waits=[(s_av, avt)], inc=s_oe)
                            b.op("scalar", lambda s: s.activation(
                                out=rC[:], in_=lntmp[:], func=AF.Exp, scale=-1.0),
                                waits=[(s_oe, b.tok(s_oe))], inc=s_oe)
                            oet2 = b.tok(s_oe)
                            def omul(v, h=h, q0=q0):
                                v.tensor_tensor(
                                    out=o_all[h][:, q0:q0 + 512], in0=pO[0][:],
                                    in1=rC[:, 0:512], op=ALU.mult)
                                return v.tensor_tensor(
                                    out=o_all[h][:, q0 + 512:q0 + 1024],
                                    in0=pO[1][:], in1=rC[:, 512:1024],
                                    op=ALU.mult)
                            b.op("vector", omul,
                                 waits=[(s_oe, oet2), (s_av, avt)], inc=s_of)
            of_all = b.tok(s_of)

            with ExitStack() as a4s:
                p4 = lambda name, shape, dtp: a4s.enter_context(
                    nc.sbuf_tensor(tag + name, shape, dtp))
                wosb = [p4(f"wos{i}", [128, HN * 256], BF16) for i in range(2)]
                yosb = [p4(f"yos{i}", [128, 256], F32) for i in range(2)]
                borep = p4("borep", [128, D], BF16)
                b.op("gpsimd", lambda g: g.dma_start(out=borep[:], in_=dt["bo_r"]),
                     waits=[(s_of, of_all)], inc=s_wo + "b", inc_n=16)
                wob = (s_wo + "b", b.tok(s_wo + "b"))
                with (
                    nc.psum_tensor(tag + "pyo0", [128, 512], F32) as pyo0,
                    nc.psum_tensor(tag + "pyo1", [128, 512], F32) as pyo1,
                ):
                    pyo = [pyo0, pyo1]
                    for fb in range(16):
                        p = fb % 2
                        fsl = slice(fb * 256, (fb + 1) * 256)
                        b.op("gpsimd", lambda g, fb=fb, p=p: g.dma_start(
                            out=wosb[p][:].rearrange("p (j f) -> p j f", j=HN),
                            in_=dt["wo_t"][fb].rearrange("(j p) f -> p j f", p=128)),
                            waits=[(s_of, of_all), (s_yo, NT * (fb - 1))],
                            inc=s_wo + str(p), inc_n=16)
                        wot = (s_wo + str(p), b.tok(s_wo + str(p)))
                        for t in range(NT):
                            nzy = fb * NT + t
                            pp = t % 2
                            tsl = slice(t * 128, (t + 1) * 128)

                            def yomm(t_, p=p, pp=pp, tsl=tsl, fsl=fsl):
                                for k in range(HN):
                                    t_.matmul(out=pyo[pp][:, 0:256],
                                              lhsT=o_all[k][:, tsl],
                                              rhs=wosb[p][:, k * 256:(k + 1) * 256],
                                              start=(k == 0), stop=False)
                                return t_.matmul(out=pyo[pp][:, 0:256],
                                                 lhsT=e0_bf[:],
                                                 rhs=borep[:, fsl], start=False,
                                                 stop=True)
                            b.op("tensor", yomm,
                                 waits=[wot, wob, (s_of, of_all),
                                        (s_ye, nzy - 1)],
                                 inc=s_yo)
                            yot = b.tok(s_yo)
                            b.op("scalar", lambda s, pp=pp: s.activation(
                                out=yosb[pp][:], in_=pyo[pp][:, 0:256], func=AF.Identity),
                                waits=[(s_yo, yot),
                                       (s_acc + str(pp), 16 * (nzy // 2))],
                                inc=s_ye)
                            yet = b.tok(s_ye)
                            b.op("gpsimd", lambda g, tsl=tsl, fsl=fsl, pp=pp:
                                 g.dma_start(out=dst[tsl, fsl], in_=yosb[pp][:],
                                             accum_op=ALU.add),
                                 waits=[(s_ye, yet)], inc=s_acc + str(pp), inc_n=16)
        return b.abs_waits([(s_acc + "0", b.tok(s_acc + "0")),
                            (s_acc + "1", b.tok(s_acc + "1"))])

    if N_SUBLAYERS == 1:
        a1w = ffn_phase(dt["x"], dt["out"], "A", [])
        b.barrier()
        b.op("gpsimd", lambda g: None, waits=a1w)
    elif N_SUBLAYERS == 2:
        a1w = ffn_phase(dt["x"], dt["X1"], "A", [])
        b.barrier()
        a2w = attn_phase(dt["X1"], dt["X1"], "B", a1w)
        b.barrier()
        with nc.sbuf_tensor("cp", [128, D], F32) as cp:
            for t in range(NT):
                b.op("sync", lambda s, t=t: s.dma_start(
                    out=cp[:], in_=dt["X1"][t * 128:(t + 1) * 128, :]),
                    waits=a2w + [("cpw", 16 * t)], inc="cpr", inc_n=16)
                b.op("sync", lambda s, t=t: s.dma_start(
                    out=dt["out"][t * 128:(t + 1) * 128, :], in_=cp[:]),
                    waits=[("cpr", 16 * (t + 1))], inc="cpw", inc_n=16)
            b.op("sync", lambda s: None, waits=[("cpw", 16 * NT)])
    else:
        a1w = ffn_phase(dt["x"], dt["X1"], "A", [])
        b.barrier()
        a2w = attn_phase(dt["X1"], dt["X1"], "B", a1w)
        b.barrier()
        a3w = ffn_phase(dt["X1"], dt["out"], "C", a2w)
        b.barrier()
        b.op("gpsimd", lambda g: None, waits=a3w)

    b.emit()
    es.close()
    return nc


_NC_CACHE = None


def _get_nc():
    global _NC_CACHE
    if _NC_CACHE is None:
        _NC_CACHE = build_program()
    return _NC_CACHE


def stage_inputs(inputs):
    ii = {k: np.asarray(v) for k, v in inputs.items()}
    C = lambda a: np.ascontiguousarray(np.asarray(a, dtype=np.float32))

    wq_t = C(ii["L0_attn_Wq"].T.reshape(D, HN, HD).transpose(1, 0, 2))
    wk_t = C(ii["L0_attn_Wk"].T.reshape(D, HN, HD).transpose(1, 0, 2))
    wv_t = C(ii["L0_attn_Wv"].T.reshape(D, HN, HD).transpose(1, 0, 2))
    wo_t = C(ii["L0_attn_Wo"].T.reshape(D, 16, 256).transpose(1, 0, 2))
    bq = C(ii["L0_attn_bq"].reshape(HN, 128).T)
    bk = C(ii["L0_attn_bk"].reshape(HN, 128).T)
    bv = C(ii["L0_attn_bv"].reshape(HN, 128).T)
    bo_r = C(np.tile(np.asarray(ii["L0_attn_bo"], np.float32).reshape(1, D),
                     (128, 1)))
    w1_t = C(ii["L0_ffn_w1q"].T.reshape(1024, NM, 128).transpose(1, 0, 2))
    w3_t = C(ii["L0_ffn_w3q"].T.reshape(1024, NM, 128).transpose(1, 0, 2))
    w2_t = np.zeros((NK2 * 128, D), np.float32)
    w2_t[:2752] = np.asarray(ii["L0_ffn_w2q"], dtype=np.float32).T
    a1_t = np.zeros((D, 128), np.float32)
    a1_t[:, 0:2] = np.asarray(ii["L0_ffn_a1"], np.float32).T
    a3_t = np.zeros((D, 128), np.float32)
    a3_t[:, 0:2] = np.asarray(ii["L0_ffn_a3"], np.float32).T
    a2f = np.asarray(ii["L0_ffn_a2"], dtype=np.float32).T
    a2_t = np.zeros((88 * 128, 128), np.float32)
    for c in range(4):
        for kk in range(NK2):
            rows = 64 if kk == NK2 - 1 else 128
            r0 = 2752 * c + 128 * kk
            blk = c * NK2 + kk
            a2_t[blk * 128:blk * 128 + rows, 0:2] = a2f[r0:r0 + rows]
    b1_t = np.zeros((128, FF), np.float32)
    b1_t[0:2] = np.asarray(ii["L0_ffn_b1"], np.float32).T
    b3_t = np.zeros((128, FF), np.float32)
    b3_t[0:2] = np.asarray(ii["L0_ffn_b3"], np.float32).T
    b2_t = np.zeros((128, D), np.float32)
    b2_t[0:2] = np.asarray(ii["L0_ffn_b2"], np.float32).T

    return dict(wq_t=wq_t, wk_t=wk_t, wv_t=wv_t, wo_t=wo_t, bq=bq, bk=bk,
                bv=bv, bo_r=bo_r, w1_t=w1_t, w3_t=w3_t, w2_t=w2_t, a1_t=a1_t,
                a3_t=a3_t, a2_t=a2_t, b1_t=b1_t, b3_t=b3_t, b2_t=b2_t)


def kernel(**inputs):
    shared = stage_inputs(inputs)
    x = np.asarray(inputs["L0_attn_x"], dtype=np.float32)
    in_maps = [dict(shared, x=np.ascontiguousarray(x[i])) for i in range(8)]

    nc = _get_nc()
    res = run_bass_kernel_spmd(nc, in_maps, core_ids=list(range(8)))
    return np.stack([np.asarray(res.results[i]["out"]) for i in range(8)], axis=0)


if __name__ == "__main__":
    import reference as R
    ins = {k: np.asarray(v) for k, v in R.setup_inputs(0).items()}
    got = kernel(**ins)
    import jax
    import jax.numpy as jnp
    if N_SUBLAYERS == 3:
        want = np.asarray(R.reference(**ins))
    else:
        x = jnp.asarray(ins["L0_attn_x"])
        ln = (x - x.mean(-1, keepdims=True)) / jnp.sqrt(x.var(-1, keepdims=True) + 1e-5)

        def ffn_ref(xx):
            LORA = 0.5
            def ql(v, wq, A, Bm):
                w = jnp.tile(wq, (1, 4))
                return v @ w.T + (v @ A.T @ Bm.T) * LORA
            g = jax.nn.silu(ql(xx, jnp.asarray(ins["L0_ffn_w1q"]), jnp.asarray(ins["L0_ffn_a1"]), jnp.asarray(ins["L0_ffn_b1"])))
            u = ql(xx, jnp.asarray(ins["L0_ffn_w3q"]), jnp.asarray(ins["L0_ffn_a3"]), jnp.asarray(ins["L0_ffn_b3"]))
            return ql(g * u, jnp.asarray(ins["L0_ffn_w2q"]), jnp.asarray(ins["L0_ffn_a2"]), jnp.asarray(ins["L0_ffn_b2"]))
        x1 = x + 0.5 * ffn_ref(ln)
        if N_SUBLAYERS == 1:
            want = np.asarray(x1)
        else:
            ln1 = (x1 - x1.mean(-1, keepdims=True)) / jnp.sqrt(x1.var(-1, keepdims=True) + 1e-5)
            B_, L_, D_, H_ = 8, 2048, 4096, 32
            q = (ln1 @ jnp.asarray(ins["L0_attn_Wq"]).T + jnp.asarray(ins["L0_attn_bq"])).reshape(B_, L_, H_, 128).transpose(0, 2, 1, 3)
            k = (ln1 @ jnp.asarray(ins["L0_attn_Wk"]).T + jnp.asarray(ins["L0_attn_bk"])).reshape(B_, L_, H_, 128).transpose(0, 2, 1, 3)
            v = (ln1 @ jnp.asarray(ins["L0_attn_Wv"]).T + jnp.asarray(ins["L0_attn_bv"])).reshape(B_, L_, H_, 128).transpose(0, 2, 1, 3)
            sc = jnp.einsum("bhqd,bhkd->bhqk", q, k) / math.sqrt(128)
            causal = jnp.tril(jnp.ones((L_, L_), bool))
            sc = jnp.where(causal, sc, -jnp.inf)
            at = jax.nn.softmax(sc, axis=-1)
            o = jnp.einsum("bhqk,bhkd->bhqd", at, v).transpose(0, 2, 1, 3).reshape(B_, L_, D_)
            want = np.asarray(x1 + o @ jnp.asarray(ins["L0_attn_Wo"]).T + jnp.asarray(ins["L0_attn_bo"]))
    d = got - want
    rv = float((d ** 2).mean() / (want ** 2).mean())
    print("residual variance ratio:", rv)
    print("max abs err:", float(np.abs(d).max()))


# version 08
# speedup vs baseline: 1.0945x; 1.0945x over previous
import math
from contextlib import ExitStack

import numpy as np

import concourse.bass as bass
import concourse.mybir as mybir
from concourse.bass_utils import run_bass_kernel_spmd

F32 = mybir.dt.float32
BF16 = mybir.dt.bfloat16
AF = mybir.ActivationFunctionType
ALU = mybir.AluOpType
AX = mybir.AxisListType

T = 2048
D = 4096
HN = 32
HD = 128
FF = 11008
NT = T // 128
NJ = D // 128
NM = FF // 128
NK2 = 22
ISQ = 1.0 / math.sqrt(HD)
EPS = 1e-5
EXPB = -10.0

ENGINES = ("tensor", "scalar", "vector", "gpsimd", "sync")
N_SUBLAYERS = 3
FFN_STAGES = 5
F2_ALOAD = True
F3_GMM = True
F3_NM = NM
F3_LORA = True
SERIAL = 1
ZSER = 2
F3_SIL = True
F3_ZW = True
F2_UMM = True
F2_EVAC = True


class B:

    def __init__(self, nc, es):
        self.nc = nc
        self.es = es
        self.ops = {e: [] for e in ENGINES}
        self.sems = {}
        self.cnt = {}
        self.base = {}
        self.lastwait = {e: {} for e in ENGINES}
        self.nbar = 0

    def sem(self, name):
        if name not in self.sems:
            self.sems[name] = self.es.enter_context(self.nc.semaphore(name))
            self.cnt[name] = 0
        return self.sems[name]

    def new_phase(self):
        self.base = dict(self.cnt)

    def tok(self, name):
        return self.cnt.get(name, 0) - self.base.get(name, 0)

    def abs_waits(self, waits):
        return [(n, self.base.get(n, 0) + v, "abs") for (n, v) in waits]

    def op(self, eng, body, waits=(), inc=None, inc_n=1):
        wl = []
        for w in waits:
            if len(w) == 3:
                n, v, _ = w
            else:
                n, v = w
                if v is None or v <= 0:
                    continue
                v = self.base.get(n, 0) + v
            if v is None or v <= 0:
                continue
            if v > self.lastwait[eng].get(n, 0):
                self.lastwait[eng][n] = v
                wl.append((self.sem(n), v))
        it = None
        tok = None
        if inc is not None:
            s = self.sem(inc)
            self.cnt[inc] = self.cnt.get(inc, 0) + inc_n
            tok = self.cnt[inc] - self.base.get(inc, 0)
            it = (s, inc_n)

        def fn(E, wl=wl, body=body, it=it):
            for s, v in wl:
                E.wait_ge(s, v)
            inst = body(E)
            if it is not None:
                if inst is None:
                    inst = E.nop()
                inst.then_inc(*it)

        self.ops[eng].append(fn)
        return tok

    def barrier(self):
        self.nbar += 1
        target = 5 * self.nbar
        s = self.sem("bar")
        for e in ENGINES:
            def fn(E, s=s, target=target):
                E.nop().then_inc(s, 1)
                E.wait_ge(s, target)
            self.ops[e].append(fn)
            self.lastwait[e]["bar"] = target

    def emit(self):
        with self.nc.Block() as block:
            @block.tensor
            def _(E):
                for fn in self.ops["tensor"]:
                    fn(E)

            @block.scalar
            def _(E):
                for fn in self.ops["scalar"]:
                    fn(E)

            @block.vector
            def _(E):
                for fn in self.ops["vector"]:
                    fn(E)

            @block.gpsimd
            def _(E):
                for fn in self.ops["gpsimd"]:
                    fn(E)

            @block.sync
            def _(E):
                for fn in self.ops["sync"]:
                    fn(E)


def build_program():
    nc = bass.Bass("TRN2", target_bir_lowering=False, debug=False,
                   detect_race_conditions=False)
    es = ExitStack()
    b = B(nc, es)

    dt = {}
    dt["x"] = nc.dram_tensor("x", [T, D], F32, kind="ExternalInput").ap()
    dt["wq_t"] = nc.dram_tensor("wq_t", [HN, D, HD], F32, kind="ExternalInput").ap()
    dt["wk_t"] = nc.dram_tensor("wk_t", [HN, D, HD], F32, kind="ExternalInput").ap()
    dt["wv_t"] = nc.dram_tensor("wv_t", [HN, D, HD], F32, kind="ExternalInput").ap()
    dt["wo_t"] = nc.dram_tensor("wo_t", [16, D, 256], F32, kind="ExternalInput").ap()
    dt["bq"] = nc.dram_tensor("bq", [128, HN], F32, kind="ExternalInput").ap()
    dt["bk"] = nc.dram_tensor("bk", [128, HN], F32, kind="ExternalInput").ap()
    dt["bv"] = nc.dram_tensor("bv", [128, HN], F32, kind="ExternalInput").ap()
    dt["w1_t"] = nc.dram_tensor("w1_t", [NM, 1024, 128], F32, kind="ExternalInput").ap()
    dt["w3_t"] = nc.dram_tensor("w3_t", [NM, 1024, 128], F32, kind="ExternalInput").ap()
    dt["w2_t"] = nc.dram_tensor("w2_t", [NK2 * 128, D], F32, kind="ExternalInput").ap()
    dt["a1_t"] = nc.dram_tensor("a1_t", [D, 128], F32, kind="ExternalInput").ap()
    dt["a3_t"] = nc.dram_tensor("a3_t", [D, 128], F32, kind="ExternalInput").ap()
    dt["a2_t"] = nc.dram_tensor("a2_t", [88 * 128, 128], F32, kind="ExternalInput").ap()
    dt["b1_t"] = nc.dram_tensor("b1_t", [128, FF], F32, kind="ExternalInput").ap()
    dt["b3_t"] = nc.dram_tensor("b3_t", [128, FF], F32, kind="ExternalInput").ap()
    dt["b2_t"] = nc.dram_tensor("b2_t", [128, D], F32, kind="ExternalInput").ap()
    dt["bo_r"] = nc.dram_tensor("bo_r", [128, D], F32, kind="ExternalInput").ap()

    dt["out"] = nc.dram_tensor("out", [T, D], F32, kind="ExternalOutput").ap()

    dt["LN"] = nc.dram_tensor("LN", [T, D], BF16, kind="Internal").ap()
    dt["X1"] = nc.dram_tensor("X1", [T, D], F32, kind="Internal").ap()
    dt["Z"] = nc.dram_tensor("Z", [NM + 1, 128, T], BF16, kind="Internal").ap()
    dt["QS"] = nc.dram_tensor("QS", [HN, 128, T], BF16, kind="Internal").ap()
    dt["KS"] = nc.dram_tensor("KS", [HN, 128, T], BF16, kind="Internal").ap()
    dt["VS"] = nc.dram_tensor("VS", [HN, 128, T], BF16, kind="Internal").ap()

    sb = lambda name, shape, dtp: es.enter_context(nc.sbuf_tensor(name, shape, dtp))

    it_f = sb("it_f", [128, 128], F32)
    ident = sb("ident", [128, 128], BF16)
    maskT = sb("maskT", [128, 128], F32)
    ones128 = sb("ones128", [128, 128], BF16)
    ident64 = sb("ident64", [128, 128], BF16)
    e0_bf = sb("e0_bf", [128, 128], BF16)
    bqs_sb = sb("bqs_sb", [128, HN], F32)
    bk_sb = sb("bk_sb", [128, HN], F32)
    bv_sb = sb("bv_sb", [128, HN], F32)
    u1_bf = sb("u1_bf", [128, T], BF16)
    u3_bf = sb("u3_bf", [128, T], BF16)
    u2_bf = sb("u2_bf", [128, T], BF16)
    eps_t = sb("eps_t", [128, 1], F32)
    expb_t = sb("expb_t", [128, 1], F32)
    zzero = sb("zzero", [128, T], BF16)

    b.op("gpsimd", lambda g: g.iota(it_f[:], [[1, 128]], channel_multiplier=-1,
                                    allow_small_or_imprecise_dtypes=True), inc="p0g")
    t0 = b.tok("p0g")

    def mk_consts(v):
        v.tensor_single_scalar(out=maskT[:], in_=it_f[:], scalar=0.0, op=ALU.is_lt)
        v.tensor_scalar_mul(out=maskT[:], in0=maskT[:], scalar1=-1e30)
        v.tensor_single_scalar(out=it_f[:], in_=it_f[:], scalar=0.0, op=ALU.is_equal)
        v.tensor_copy(out=ident[:], in_=it_f[:])
        v.memset(ident64[:], 0.0)
        v.tensor_copy(out=ident64[0:64, :], in_=ident[0:64, :])
        v.memset(ones128[:], 1.0)
        v.memset(eps_t[:], EPS)
        v.memset(expb_t[:], EXPB)
        v.memset(u1_bf[:], 0.0)
        v.memset(u3_bf[:], 0.0)
        v.memset(zzero[:], 0.0)
        return v.memset(u2_bf[:], 0.0)
    b.op("vector", mk_consts, waits=[("p0g", t0)], inc="p0v")
    b.op("gpsimd", lambda g: g.iota(it_f[:], [[0, 128]], channel_multiplier=1,
                                    allow_small_or_imprecise_dtypes=True),
         waits=[("p0v", 1)], inc="p0g")
    t0b = b.tok("p0g")
    def mk_e0(v):
        v.tensor_single_scalar(out=it_f[:], in_=it_f[:], scalar=0.0, op=ALU.is_equal)
        return v.tensor_copy(out=e0_bf[:], in_=it_f[:])
    b.op("vector", mk_e0, waits=[("p0g", t0b)], inc="p0v")

    b.op("sync", lambda s: s.dma_start(out=bqs_sb[:], in_=dt["bq"]), inc="p0e", inc_n=16)
    b.op("sync", lambda s: s.dma_start(out=bk_sb[:], in_=dt["bk"]), inc="p0e", inc_n=16)
    b.op("sync", lambda s: s.dma_start(out=bv_sb[:], in_=dt["bv"]), inc="p0e", inc_n=16)
    t1 = b.tok("p0e")
    b.op("vector", lambda v: v.tensor_scalar_mul(out=bqs_sb[:], in0=bqs_sb[:], scalar1=ISQ),
         waits=[("p0e", t1)], inc="p0v")
    b.barrier()

    def ln_pass(src, tag, ph, ready, prefill_dst=None):
        s_x, s_st, s_rs, s_ln = "Lx", "Lst", "Lrs", "Lln"
        s_lnw, s_pref, s_sq2 = "Llnw", "Lpf", "Lsq2"
        psb = lambda name, shape, dtp: ph.enter_context(
            nc.sbuf_tensor(tag + name, shape, dtp))
        xt = [psb(f"xt{i}", [128, D], F32) for i in range(4)]
        sq = psb("sq", [128, D], F32)
        lnt = [psb(f"lnt{i}", [128, D], BF16) for i in range(2)]
        stm = [psb(f"stm{i}", [128, 1], F32) for i in range(2)]
        stv = [psb(f"stv{i}", [128, 1], F32) for i in range(2)]
        str_ = [psb(f"str{i}", [128, 1], F32) for i in range(2)]

        def emit_load(tl):
            p4 = tl % 4
            w = [(s_ln, tl - 3)]
            if prefill_dst is not None and tl >= 4:
                w.append((s_pref + str((tl - 4) % 2), 16 * ((tl - 4) // 2 + 1)))
            b.op("sync" if tl % 2 == 0 else "scalar",
                 lambda s, tl=tl, p4=p4: s.dma_start(
                     out=xt[p4][:], in_=src[tl * 128:(tl + 1) * 128, :]),
                 waits=list(ready) + w, inc=s_x + str(tl % 2), inc_n=16)
            return (s_x + str(tl % 2), b.tok(s_x + str(tl % 2)))

        def emit_back(t, p, p4, rst):
            b.op("vector", lambda v, p=p, p4=p4: v.tensor_scalar(
                out=lnt[p][:], in0=xt[p4][:], scalar1=stm[p][:],
                scalar2=str_[p][:], op0=ALU.subtract, op1=ALU.mult),
                waits=[(s_rs, rst), (s_lnw + str(p), 16 * (t // 2))],
                inc=s_ln)
            b.op("sync" if p == 0 else "scalar",
                 lambda s, t=t, p=p: s.dma_start(
                     out=dt["LN"][t * 128:(t + 1) * 128, :], in_=lnt[p][:]),
                 waits=[(s_ln, b.tok(s_ln))], inc=s_lnw + str(p), inc_n=16)

        xls = [emit_load(0), emit_load(1)]
        pend = []
        for t in range(NT):
            p = t % 2
            p4 = t % 4
            if t + 2 < NT:
                xls.append(emit_load(t + 2))
            xl = xls[t]
            if prefill_dst is not None:
                b.op("gpsimd", lambda g, t=t, p4=p4: g.dma_start(
                    out=prefill_dst[t * 128:(t + 1) * 128, :], in_=xt[p4][:]),
                    waits=[xl], inc=s_pref + str(p), inc_n=16)

            b.op("scalar", lambda s, p=p, p4=p4: s.activation(
                out=sq[:], in_=xt[p4][:], func=AF.Square,
                accum_out=stv[p][:]),
                waits=[xl, (s_st, 3 * (t - 1))], inc=s_sq2)
            sq2t = b.tok(s_sq2)
            b.op("vector", lambda v, p=p, p4=p4: v.reduce_sum(
                out=stm[p][:], in_=xt[p4][:], axis=AX.X),
                waits=[xl], inc=s_st)
            def stats2(v, p=p):
                v.tensor_scalar_mul(out=stm[p][:], in0=stm[p][:], scalar1=1.0 / D)
                return v.tensor_scalar_mul(out=stv[p][:], in0=stv[p][:],
                                           scalar1=1.0 / D)
            b.op("vector", stats2, waits=[(s_st, b.tok(s_st)), (s_sq2, sq2t)],
                 inc=s_st)
            b.op("vector", lambda v, p=p: v.scalar_tensor_tensor(
                out=stv[p][:], in0=stm[p][:], scalar=stm[p][:], in1=stv[p][:],
                op0=ALU.mult, op1=ALU.subtract),
                waits=[(s_st, b.tok(s_st))], inc=s_st)

            b.op("scalar", lambda s, p=p: s.activation(
                out=str_[p][:], in_=stv[p][:], func=AF.Ln,
                scale=-1.0, bias=eps_t[:]),
                waits=[(s_st, b.tok(s_st))], inc=s_rs)
            b.op("scalar", lambda s, p=p: s.activation(
                out=str_[p][:], in_=str_[p][:], func=AF.Exp, scale=-0.5),
                waits=[(s_rs, b.tok(s_rs))], inc=s_rs)
            pend.append((t, p, p4, b.tok(s_rs)))
            if len(pend) > 1:
                emit_back(*pend.pop(0))
        while pend:
            emit_back(*pend.pop(0))
        lnw_w = [(s_lnw + "0", b.tok(s_lnw + "0")), (s_lnw + "1", b.tok(s_lnw + "1"))]
        pref_w = ([(s_pref + "0", b.tok(s_pref + "0")),
                   (s_pref + "1", b.tok(s_pref + "1"))]
                  if prefill_dst is not None else [])
        return lnw_w, pref_w

    def ffn_phase(src, dst, tag, ready):
        b.new_phase()
        s_lnr = "Flnr"
        s_u = "Fu"
        s_ue = "Fue"
        s_xs = "Fxs"
        s_xc = "Fxc"
        s_w = "Fw"
        s_pw = "Fpw"
        s_g = "Fg"
        s_si = "Fsi"
        s_z = "Fz"
        s_zw = "Fzw"
        s_zr = "Fzr"
        s_zs = "Fzs"
        s_zc = "Fzc"
        s_u2 = "Fu2"
        s_u2e = "Fu2e"
        s_w2 = "Fw2"
        s_y = "Fy"
        s_ye = "Fye"
        s_acc = "Facc"

        with ExitStack() as ph:
            psb = lambda name, shape, dtp: ph.enter_context(
                nc.sbuf_tensor(tag + name, shape, dtp))

            swa = ExitStack()
            pwa = lambda name, shape, dtp: swa.enter_context(
                nc.sbuf_tensor(tag + name, shape, dtp))
            a1t_sb = pwa("a1t", [128, HN * 128], BF16)
            a3t_sb = pwa("a3t", [128, HN * 128], BF16)
            b1sb = pwa("b1s", [128, FF], BF16)
            b3sb = pwa("b3s", [128, FF], BF16)
            if F2_ALOAD:
                b.op("gpsimd", lambda g: g.dma_start(
                    out=a1t_sb[:].rearrange("p (j f) -> p j f", j=HN),
                    in_=dt["a1_t"].rearrange("(j p) f -> p j f", p=128)),
                    waits=list(ready), inc="Faw", inc_n=16)
                b.op("gpsimd", lambda g: g.dma_start(
                    out=a3t_sb[:].rearrange("p (j f) -> p j f", j=HN),
                    in_=dt["a3_t"].rearrange("(j p) f -> p j f", p=128)),
                    inc="Faw", inc_n=16)
            aw_all = b.tok("Faw")
            b.op("gpsimd", lambda g: g.dma_start(out=b1sb[:], in_=dt["b1_t"]),
                 waits=list(ready), inc="Fwb", inc_n=16)
            b.op("gpsimd", lambda g: g.dma_start(out=b3sb[:], in_=dt["b3_t"]),
                 inc="Fwb", inc_n=16)
            wb = ("Fwb", b.tok("Fwb"))

            with ExitStack() as s1:
                lnw_w, pref_w = ln_pass(src, tag + "a", s1, ready, prefill_dst=dst)
            if FFN_STAGES <= 1:
                b.op("sync", lambda s: None, waits=pref_w + lnw_w)
                return b.abs_waits(pref_w)

            sx = ExitStack()
            xs_bf = [sx.enter_context(nc.sbuf_tensor(tag + f"xsb{i}", [128, T], BF16))
                     for i in range(8)]

            with ExitStack() as s2:
                p2 = lambda name, shape, dtp: s2.enter_context(
                    nc.sbuf_tensor(tag + name, shape, dtp))
                lnf = [p2(f"lnf{i}", [128, T], BF16) for i in range(3)]
                xs_acc = [p2(f"xsa{i}", [128, T], F32) for i in range(8)]
                with ExitStack() as pst:
                    pu1 = [pst.enter_context(nc.psum_tensor(tag + f"pu1_{i}",
                           [128, 512], F32)) for i in range(4)]
                    pu3 = [pst.enter_context(nc.psum_tensor(tag + f"pu3_{i}",
                           [128, 512], F32)) for i in range(4)]
                    for j in range(NJ):
                        p = j % 3
                        b.op("sync", lambda s, j=j, p=p: s.dma_start_transpose(
                            out=lnf[p][:], in_=dt["LN"][:, j * 128:(j + 1) * 128]),
                            waits=lnw_w + pref_w + [(s_xs, j - 2)] +
                                  ([(s_u, 8 * (j - 2))] if F2_UMM else []),
                            inc=s_lnr + str(p), inc_n=16)
                        lr = (s_lnr + str(p), b.tok(s_lnr + str(p)))

                        def umm(t_, j=j, p=p):
                            last = None
                            for fb in range(4):
                                t_.matmul(out=pu1[fb][:],
                                          lhsT=a1t_sb[:, j * 128:(j + 1) * 128],
                                          rhs=lnf[p][:, fb * 512:(fb + 1) * 512],
                                          start=(j == 0), stop=(j == NJ - 1))
                            for fb in range(4):
                                last = t_.matmul(out=pu3[fb][:],
                                          lhsT=a3t_sb[:, j * 128:(j + 1) * 128],
                                          rhs=lnf[p][:, fb * 512:(fb + 1) * 512],
                                          start=(j == 0), stop=(j == NJ - 1))
                            return last
                        if F2_UMM:
                            b.op("tensor", umm,
                                 waits=[lr, ("Faw", aw_all)],
                                 inc=s_u, inc_n=8)
                        jp = j % 8
                        if j < 8:
                            b.op("vector", lambda v, jp=jp, p=p: v.tensor_copy(
                                out=xs_acc[jp][:], in_=lnf[p][:]),
                                waits=[lr], inc=s_xs)
                        else:
                            b.op("vector", lambda v, jp=jp, p=p: v.tensor_tensor(
                                out=xs_acc[jp][:], in0=xs_acc[jp][:], in1=lnf[p][:],
                                op=ALU.add),
                                waits=[lr], inc=s_xs)
                    xs_all = b.tok(s_xs)
                    for jp in range(8):
                        b.op("vector", lambda v, jp=jp: v.tensor_copy(
                            out=xs_bf[jp][:], in_=xs_acc[jp][:]),
                            waits=[(s_xs, xs_all)], inc=s_xc)
                    u_all = b.tok(s_u)
                    if F2_EVAC:
                        def uev(s):
                            last = None
                            for fb in range(4):
                                s.activation(out=u1_bf[:, fb * 512:(fb + 1) * 512],
                                             in_=pu1[fb][:], func=AF.Identity,
                                             scale=0.5)
                                last = s.activation(
                                    out=u3_bf[:, fb * 512:(fb + 1) * 512],
                                    in_=pu3[fb][:], func=AF.Identity, scale=0.5)
                            return last
                        b.op("scalar", uev, waits=[(s_u, u_all)], inc=s_ue)
                        b.op("scalar", lambda s: None,
                             waits=[(s_ue, b.tok(s_ue))], inc=s_ue)
            ue_all = b.tok(s_ue)
            xc_all = b.tok(s_xc)
            if FFN_STAGES <= 2:
                b.op("sync", lambda s: None,
                     waits=[(s_xc, xc_all), (s_ue, ue_all)] + pref_w)
                sx.close()
                swa.close()
                return b.abs_waits(pref_w)

            with ExitStack() as s3:
                p3 = lambda name, shape, dtp: s3.enter_context(
                    nc.sbuf_tensor(tag + name, shape, dtp))
                w1sb = [p3(f"w1s{i}", [128, 1024], BF16) for i in range(2)]
                w3sb = [p3(f"w3s{i}", [128, 1024], BF16) for i in range(2)]
                sil = [p3(f"sil{i}", [128, 512], F32) for i in range(4)]
                zst = [p3(f"zst{i}", [128, T], BF16) for i in range(2)]
                gate2 = [(s_xc, xc_all), (s_ue, ue_all)]
                with ExitStack() as pgs:
                    pgt = [pgs.enter_context(nc.psum_tensor(tag + f"pg{i}",
                           [128, 512], F32)) for i in range(8)]
                    pg = [(pgt[0], pgt[1]), (pgt[2], pgt[3]),
                          (pgt[4], pgt[5]), (pgt[6], pgt[7])]
                    for m in range(F3_NM):
                        p = m % 2
                        b.op("gpsimd", lambda g, m=m, p=p: g.dma_start(
                            out=w1sb[p][:].rearrange("p (k f) -> p k f", k=8),
                            in_=dt["w1_t"][m].rearrange("(k p) f -> p k f", p=128)),
                            waits=[(s_pw, m - 1)] + (gate2 if m < 2 else []),
                            inc=s_w + str(p), inc_n=16)
                        b.op("gpsimd", lambda g, m=m, p=p: g.dma_start(
                            out=w3sb[p][:].rearrange("p (k f) -> p k f", k=8),
                            in_=dt["w3_t"][m].rearrange("(k p) f -> p k f", p=128)),
                            inc=s_w + str(p), inc_n=16)
                        wt = (s_w + str(p), b.tok(s_w + str(p)))
                        for tq in range(4):
                            nz = 4 * m + tq
                            pp = nz % 4
                            g1p, g3p = pg[pp]
                            sl = slice(tq * 512, (tq + 1) * 512)
                            if not F3_GMM:
                                continue

                            def gmm(t_, m=m, p=p, sl=sl, g1p=g1p, g3p=g3p):
                                for k in range(8):
                                    t_.matmul(out=g1p[:],
                                              lhsT=w1sb[p][:, k * 128:(k + 1) * 128],
                                              rhs=xs_bf[k][:, sl],
                                              start=(k == 0), stop=not (F3_LORA or k < 7))
                                if F3_LORA:
                                    t_.matmul(out=g1p[:],
                                              lhsT=b1sb[:, m * 128:(m + 1) * 128],
                                              rhs=u1_bf[:, sl], start=False, stop=True)
                                for k in range(8):
                                    last = t_.matmul(out=g3p[:],
                                              lhsT=w3sb[p][:, k * 128:(k + 1) * 128],
                                              rhs=xs_bf[k][:, sl],
                                              start=(k == 0), stop=not (F3_LORA or k < 7))
                                if F3_LORA:
                                    last = t_.matmul(out=g3p[:],
                                          lhsT=b3sb[:, m * 128:(m + 1) * 128],
                                          rhs=u3_bf[:, sl], start=False, stop=True)
                                return last
                            b.op("tensor", gmm,
                                 waits=[wt, wb, (s_xc, xc_all), (s_ue, ue_all)] +
                                       ([(s_z, nz - 3)] if F3_SIL else
                                        [(s_g, nz - 1)]),
                                 inc=s_g)
                            gt = b.tok(s_g)
                            if not F3_SIL:
                                continue
                            b.op("scalar", lambda s, pp=pp, g1p=g1p: s.activation(
                                out=sil[pp][:], in_=g1p[:], func=AF.Silu),
                                waits=[(s_g, gt), (s_z, nz - 3)], inc=s_si)
                            sit = b.tok(s_si)
                            b.op("vector", lambda v, pp=pp, p=p, sl=sl, g3p=g3p:
                                 v.tensor_tensor(out=zst[p][:, sl], in0=sil[pp][:],
                                                 in1=g3p[:], op=ALU.mult),
                                 waits=[(s_si, sit)] +
                                       ([(s_zw + str(p), 16 * (m // 2))]
                                        if F3_ZW else []),
                                 inc=s_z)
                        zt = b.tok(s_z)
                        b.op("tensor", lambda t_: None,
                             waits=[(s_g, b.tok(s_g))], inc=s_pw)
                        if F3_ZW and F3_SIL:
                            b.op("sync", lambda s, m=m, p=p: s.dma_start(
                                out=dt["Z"][m], in_=zst[p][:]),
                                waits=[(s_z, zt)], inc=s_zw + str(p), inc_n=16)
                zw_w = [(s_zw + "0", b.tok(s_zw + "0")),
                        (s_zw + "1", b.tok(s_zw + "1"))]
            sx.close()
            swa.close()
            if FFN_STAGES <= 3:
                b.op("sync", lambda s: None,
                     waits=zw_w + [(s_z, b.tok(s_z)), (s_g, b.tok(s_g))] + pref_w)
                return b.abs_waits(pref_w)

            zs_bf = [psb(f"zsb{i}", [128, T], BF16) for i in range(NK2)]
            zflat = dt["Z"].rearrange("m p t -> (m p) t")
            b.op("sync", lambda s: s.dma_start(out=dt["Z"][NM], in_=zzero[:]),
                 waits=[("p0v", 3, "abs")], inc="Fzp", inc_n=16)
            fzp = ("Fzp", b.tok("Fzp"))
            with ExitStack() as s45:
                p45 = lambda name, shape, dtp: s45.enter_context(
                    nc.sbuf_tensor(tag + name, shape, dtp))
                a2t_sb = p45("a2t", [128, 88 * 128], BF16)
                zld = [p45(f"zld{i}", [128, T], BF16) for i in range(5)]
                w2sb = [p45(f"w2s{i}", [128, NK2 * 512], BF16) for i in range(2)]
                b2sb = p45("b2s", [128, D], BF16)
                ysb = [p45(f"ysb{i}", [128, 512], F32) for i in range(2)]
                b.op("gpsimd", lambda g: g.dma_start(
                    out=a2t_sb[:].rearrange("p (m f) -> p m f", m=88),
                    in_=dt["a2_t"].rearrange("(m p) f -> p m f", p=128)),
                    waits=zw_w + [fzp], inc="Fa2w", inc_n=16)
                a2w_all = b.tok("Fa2w")
                b.op("gpsimd", lambda g: g.dma_start(out=b2sb[:], in_=dt["b2_t"]),
                     waits=zw_w + [fzp], inc=s_w2 + "b", inc_n=16)
                w2b = (s_w2 + "b", b.tok(s_w2 + "b"))
                with (
                    nc.psum_tensor(tag + "pu2q0", [128, 512], F32) as pu2q0,
                    nc.psum_tensor(tag + "pu2q1", [128, 512], F32) as pu2q1,
                    nc.psum_tensor(tag + "pu2q2", [128, 512], F32) as pu2q2,
                    nc.psum_tensor(tag + "pu2q3", [128, 512], F32) as pu2q3,
                    nc.psum_tensor(tag + "zp0", [128, 512], F32) as zp0,
                    nc.psum_tensor(tag + "zp1", [128, 512], F32) as zp1,
                    nc.psum_tensor(tag + "py0", [128, 512], F32) as py0,
                    nc.psum_tensor(tag + "py1", [128, 512], F32) as py1,
                ):
                    pu2q = [pu2q0, pu2q1, pu2q2, pu2q3]
                    zp = [zp0, zp1]
                    py = [py0, py1]
                    zrs = []
                    for kk in range(NK2):
                        for c in range(4):
                            idx = kk * 4 + c
                            p = idx % 5
                            r0 = 2752 * c + 128 * kk
                            m0 = r0 // 128
                            m1 = (r0 + 127) // 128
                            lw = []
                            for mm_ in {m0, m1}:
                                if mm_ >= NM:
                                    lw.append(fzp)
                                else:
                                    lw.append((s_zw + str(mm_ % 2),
                                               16 * (mm_ // 2 + 1)))
                            if idx >= 5:
                                lw.append((s_u2, idx - 4))
                                lw.append(("Fzm", (idx - 5) // 4 + 1))
                            b.op("sync", lambda s, p=p, r0=r0:
                                 s.dma_start(out=zld[p][:],
                                             in_=zflat[r0:r0 + 128, :]),
                                 waits=lw, inc=s_zr + str(p), inc_n=16)
                            zr = (s_zr + str(p), b.tok(s_zr + str(p)))
                            zrs.append(zr)
                            blk = c * NK2 + kk
                            rows = 64 if kk == NK2 - 1 else 128

                            def u2mm(t_, p=p, blk=blk, rows=rows, idx=idx):
                                last = None
                                for q in range(4):
                                    last = t_.matmul(
                                        out=pu2q[q][:],
                                        lhsT=a2t_sb[0:rows,
                                                    blk * 128:(blk + 1) * 128],
                                        rhs=zld[p][0:rows,
                                                   q * 512:(q + 1) * 512],
                                        start=(idx == 0), stop=(idx == 87))
                                return last
                            b.op("tensor", u2mm,
                                 waits=[zr, ("Fa2w", a2w_all)], inc=s_u2)
                        idn = ident64 if kk == NK2 - 1 else ident
                        for fb in range(4):
                            ze = kk * 4 + fb
                            pz = ze % 2

                            def zsmm(t_, kk=kk, fb=fb, pz=pz, idn=idn):
                                last = None
                                for c in range(4):
                                    pbuf = (kk * 4 + c) % 5
                                    last = t_.matmul(
                                        out=zp[pz][:],
                                        lhsT=idn[:],
                                        rhs=zld[pbuf][:, fb * 512:(fb + 1) * 512],
                                        start=(c == 0), stop=(c == 3))
                                return last
                            b.op("tensor", zsmm,
                                 waits=zrs[kk * 4:kk * 4 + 4] +
                                       [(s_zc, ze - 1)],
                                 inc="Fzm" if fb == 3 else "Fzmq")
                            b.op("scalar", lambda s, kk=kk, fb=fb, pz=pz:
                                 s.activation(
                                     out=zs_bf[kk][:, fb * 512:(fb + 1) * 512],
                                     in_=zp[pz][:], func=AF.Identity),
                                 waits=[("Fzm" if fb == 3 else "Fzmq",
                                         b.tok("Fzm" if fb == 3 else "Fzmq"))],
                                 inc=s_zc)
                    u2t = b.tok(s_u2)
                    def u2ev(s):
                        last = None
                        for q in range(4):
                            last = s.activation(
                                out=u2_bf[:, q * 512:(q + 1) * 512],
                                in_=pu2q[q][:], func=AF.Identity, scale=0.5)
                        return last
                    b.op("scalar", u2ev, waits=[(s_u2, u2t)], inc=s_u2e)
                    zc_all = b.tok(s_zc)
                    u2e_all = b.tok(s_u2e)
                    if FFN_STAGES <= 4:
                        b.op("sync", lambda s: None,
                             waits=[(s_zc, zc_all), (s_u2e, u2e_all)] + pref_w)
                        return b.abs_waits(pref_w)

                    for fb in range(8):
                        p = fb % 2
                        fsl = slice(fb * 512, (fb + 1) * 512)
                        b.op("gpsimd", lambda g, p=p, fsl=fsl: g.dma_start(
                            out=w2sb[p][:].rearrange("p (k f) -> p k f", k=NK2),
                            in_=dt["w2_t"][:, fsl].rearrange("(k p) f -> p k f", p=128)),
                            waits=zw_w + [fzp, (s_y, NT * (fb - 1))],
                            inc=s_w2 + str(p), inc_n=16)
                        w2t_ = (s_w2 + str(p), b.tok(s_w2 + str(p)))
                        for t in range(NT):
                            nzy = fb * NT + t
                            pp = t % 2
                            tsl = slice(t * 128, (t + 1) * 128)

                            def ymm(t_, p=p, pp=pp, tsl=tsl, fb=fb):
                                for k in range(NK2):
                                    t_.matmul(
                                        out=py[pp][:],
                                        lhsT=zs_bf[k][:, tsl],
                                        rhs=w2sb[p][:, k * 512:(k + 1) * 512],
                                        start=(k == 0), stop=False)
                                return t_.matmul(
                                    out=py[pp][:],
                                    lhsT=u2_bf[:, tsl],
                                    rhs=b2sb[:, fb * 512:(fb + 1) * 512],
                                    start=False, stop=True)
                            b.op("tensor", ymm,
                                 waits=[w2t_, w2b, (s_zc, zc_all),
                                        (s_u2e, u2e_all), (s_ye, nzy - 1)],
                                 inc=s_y)
                            yt = b.tok(s_y)
                            b.op("scalar", lambda s, pp=pp: s.activation(
                                out=ysb[pp][:], in_=py[pp][:], func=AF.Identity,
                                scale=0.5),
                                waits=[(s_y, yt),
                                       (s_acc + str(pp), 16 * ((nzy) // 2))],
                                inc=s_ye)
                            yet = b.tok(s_ye)
                            b.op("gpsimd", lambda g, tsl=tsl, fsl=fsl, pp=pp:
                                 g.dma_start(out=dst[tsl, fsl], in_=ysb[pp][:],
                                             accum_op=ALU.add),
                                 waits=[(s_ye, yet)] + pref_w,
                                 inc=s_acc + str(pp), inc_n=16)
        return b.abs_waits([(s_acc + "0", b.tok(s_acc + "0")),
                            (s_acc + "1", b.tok(s_acc + "1"))])

    def attn_phase(src, dst, tag, ready):
        b.new_phase()
        s_lnr = "Blnr"
        s_w = "Fw"
        s_p = "Bp"
        s_pe = "Bpe"
        s_qw = "Bqw"
        s_hl = "Bhl"
        s_vt = "Bvt"
        s_ve = "Bve"
        s_stm = "Bsm"
        s_msk = "Bmk"
        s_ex = "Bex"
        s_av = "Bav"
        s_oe = "Boe"
        s_rc = "Brc"
        s_cb = "Bcb"
        s_of = "Bof"
        s_wo = "Bwo"
        s_yo = "Fy"
        s_ye = "Fye"
        s_acc = "Facc"

        with ExitStack() as ph:
            psb = lambda name, shape, dtp: ph.enter_context(
                nc.sbuf_tensor(tag + name, shape, dtp))

            with ExitStack() as a1s:
                lnw_w, _ = ln_pass(src, tag + "a", a1s, ready)

            with ExitStack() as a2s:
                p2 = lambda name, shape, dtp: a2s.enter_context(
                    nc.sbuf_tensor(tag + name, shape, dtp))
                ln2 = [p2(f"ln2_{j}", [128, T], BF16) for j in range(NJ)]
                wsb = [p2(f"wsb{i}", [128, D], BF16) for i in range(3)]
                qkv_sb = [p2(f"qkv{i}", [128, T], BF16) for i in range(2)]
                for j in range(NJ):
                    b.op("sync", lambda s, j=j: s.dma_start_transpose(
                        out=ln2[j][:], in_=dt["LN"][:, j * 128:(j + 1) * 128]),
                        waits=lnw_w, inc=s_lnr, inc_n=16)
                lnr_all = b.tok(s_lnr)
                projs = [(dt["wq_t"], dt["QS"], bqs_sb, ISQ),
                         (dt["wk_t"], dt["KS"], bk_sb, 1.0),
                         (dt["wv_t"], dt["VS"], bv_sb, 1.0)]
                with ExitStack() as pst:
                    pq = [[pst.enter_context(nc.psum_tensor(tag + f"pq{i}_{fb}",
                           [128, 512], F32)) for fb in range(4)] for i in range(2)]
                    nproj = 0
                    for h in range(HN):
                        for (wsrc, dsts, bias, scl) in projs:
                            wp = nproj % 3
                            pp = nproj % 2
                            b.op("gpsimd", lambda g, h=h, wsrc=wsrc, wp=wp:
                                 g.dma_start(
                                     out=wsb[wp][:].rearrange("p (j f) -> p j f", j=NJ),
                                     in_=wsrc[h].rearrange("(j p) f -> p j f", p=128)),
                                 waits=lnw_w + [(s_p, nproj - 2)],
                                 inc=s_w + str(wp), inc_n=16)
                            wt = (s_w + str(wp), b.tok(s_w + str(wp)))

                            def pmm(t_, wp=wp, pp=pp):
                                last = None
                                for j in range(NJ):
                                    for fb in range(4):
                                        last = t_.matmul(
                                            out=pq[pp][fb][:],
                                            lhsT=wsb[wp][:, j * 128:(j + 1) * 128],
                                            rhs=ln2[j][:, fb * 512:(fb + 1) * 512],
                                            start=(j == 0), stop=(j == NJ - 1))
                                return last
                            b.op("tensor", pmm,
                                 waits=[wt, (s_lnr, lnr_all),
                                        (s_pe, nproj - 1)],
                                 inc=s_p)
                            pt = b.tok(s_p)
                            def qev(s, pp=pp, bias=bias, scl=scl, h=h):
                                last = None
                                for fb in range(4):
                                    last = s.activation(
                                        out=qkv_sb[pp][:, fb * 512:(fb + 1) * 512],
                                        in_=pq[pp][fb][:], func=AF.Identity,
                                        bias=bias[:, h:h + 1], scale=scl)
                                return last
                            b.op("scalar", qev,
                                 waits=[(s_p, pt),
                                        (s_qw + str(pp), 16 * (nproj // 2))],
                                 inc=s_pe)
                            pet = b.tok(s_pe)
                            b.op("sync", lambda s, dsts=dsts, h=h, pp=pp:
                                 s.dma_start(out=dsts[h], in_=qkv_sb[pp][:]),
                                 waits=[(s_pe, pet)], inc=s_qw + str(pp), inc_n=16)
                            nproj += 1
            qw_w = [(s_qw + "0", b.tok(s_qw + "0")), (s_qw + "1", b.tok(s_qw + "1"))]

            o_all = [psb(f"oall{k}", [128, T], BF16) for k in range(HN)]
            with ExitStack() as a3s:
                p3 = lambda name, shape, dtp: a3s.enter_context(
                    nc.sbuf_tensor(tag + name, shape, dtp))
                qh = [p3(f"qh{i}", [128, T], BF16) for i in range(2)]
                kh = [p3(f"kh{i}", [128, T], BF16) for i in range(2)]
                vh = [p3(f"vh{i}", [128, T], BF16) for i in range(2)]
                vtm = [p3(f"vtm{i}", [128, T], BF16) for i in range(2)]
                att = [p3(f"att{i}", [128, 512], BF16) for i in range(4)]
                lntmp = p3("lntmp", [128, 1024], F32)
                rC = p3("rC", [128, 1024], F32)
                with (
                    nc.psum_tensor(tag + "ps0", [128, 512], F32) as ps0,
                    nc.psum_tensor(tag + "ps1", [128, 512], F32) as ps1,
                    nc.psum_tensor(tag + "ps2", [128, 512], F32) as ps2,
                    nc.psum_tensor(tag + "ps3", [128, 512], F32) as ps3,
                    nc.psum_tensor(tag + "pO0", [128, 512], F32) as pO0,
                    nc.psum_tensor(tag + "pO1", [128, 512], F32) as pO1,
                    nc.psum_tensor(tag + "pl0", [128, 512], F32) as pl0,
                    nc.psum_tensor(tag + "pl1", [128, 512], F32) as pl1,
                ):
                    pO = [pO0, pO1]
                    pl = [pl0, pl1]
                    ps = [ps0, ps1, ps2, ps3]
                    nblk = 0
                    for h in range(HN):
                        p = h % 2
                        b.op("sync", lambda s, h=h, p=p: s.dma_start(
                            out=qh[p][:], in_=dt["QS"][h]),
                            waits=qw_w +
                                  ([(s_of, 2 * (h - 1))] if h >= 2 else []),
                            inc=s_hl + str(p), inc_n=16)
                        b.op("sync", lambda s, h=h, p=p: s.dma_start(
                            out=kh[p][:], in_=dt["KS"][h]), inc=s_hl + str(p), inc_n=16)
                        b.op("sync", lambda s, h=h, p=p: s.dma_start(
                            out=vh[p][:], in_=dt["VS"][h]), inc=s_hl + str(p), inc_n=16)
                        hl = (s_hl + str(p), b.tok(s_hl + str(p)))
                        exp_prev = b.tok(s_ex)
                        for tt in range(NT):
                            pv = tt % 4
                            pst = ps[pv].bitcast(BF16)[:, 0:128]
                            b.op("tensor", lambda t_, p=p, tt=tt, pst=pst:
                                 t_.transpose(out=pst,
                                              in_=vh[p][:, tt * 128:(tt + 1) * 128],
                                              identity=ident[:]),
                                 waits=[hl, (s_ex, exp_prev),
                                        (s_ve, h * NT + tt - 3)],
                                 inc=s_vt)
                            vt = b.tok(s_vt)
                            b.op("scalar", lambda s, p=p, tt=tt, pst=pst:
                                 s.activation(out=vtm[p][:, tt * 128:(tt + 1) * 128],
                                              in_=pst, func=AF.Identity),
                                 waits=[(s_vt, vt)], inc=s_ve)
                        ve_all = b.tok(s_ve)
                        for half in range(2):
                            q0, q1 = half * 1024, (half + 1) * 1024
                            oe_gate = b.tok(s_oe)
                            of_gate = b.tok(s_of)
                            blocks = []
                            for j in range(16):
                                c0 = max(128 * j, q0)
                                if c0 >= q1:
                                    continue
                                cb = c0
                                while cb < q1:
                                    ce = min(((cb // 512) + 1) * 512, q1)
                                    blocks.append((j, cb, ce))
                                    cb = ce
                            SKEW = 3
                            pend = []
                            seq = list(blocks) + [None] * SKEW
                            for blk_i, blk in enumerate(seq):
                                if blk is not None:
                                    (j, cb, ce) = blk
                                    pp2 = nblk % 4
                                    diag = (cb <= 128 * j < ce)
                                    b.op("tensor", lambda t_, j=j, cb=cb, ce=ce,
                                         pp2=pp2, p=p: t_.matmul(
                                             out=ps[pp2][:, 0:ce - cb],
                                             lhsT=kh[p][:, j * 128:(j + 1) * 128],
                                             rhs=qh[p][:, cb:ce],
                                             start=True, stop=True),
                                         waits=[hl, (s_ve, ve_all),
                                                (s_ex, nblk - 3)],
                                         inc=s_stm)
                                    smt = b.tok(s_stm)
                                    if diag:
                                        off = 128 * j - cb
                                        b.op("vector", lambda v, pp2=pp2, off=off:
                                             v.tensor_tensor(
                                                 out=ps[pp2][:, off:off + 128],
                                                 in0=ps[pp2][:, off:off + 128],
                                                 in1=maskT[:], op=ALU.add),
                                             waits=[(s_stm, smt)], inc=s_msk)
                                        mw = [(s_msk, b.tok(s_msk))]
                                    else:
                                        mw = [(s_stm, smt)]
                                    pa = nblk % 4
                                    b.op("scalar", lambda s, pp2=pp2, pa=pa, cb=cb,
                                         ce=ce: s.activation(
                                             out=att[pa][:, 0:ce - cb],
                                             in_=ps[pp2][:, 0:ce - cb],
                                             func=AF.Exp, bias=expb_t[:]),
                                         waits=mw + [(s_av, nblk - 3)], inc=s_ex)
                                    pend.append((j, cb, ce, pa, b.tok(s_ex)))
                                    nblk += 1
                                if len(pend) > (SKEW if blk is not None else 0):
                                    (j, cb, ce, pa, ext) = pend.pop(0)

                                    def avmm(t_, j=j, cb=cb, ce=ce, pa=pa, p=p,
                                             q0=q0):
                                        st = (j == 0)
                                        sp = (128 * (j + 1) >= ce)
                                        r = (cb - q0) // 512
                                        o0 = cb - q0 - 512 * r
                                        o1 = ce - q0 - 512 * r
                                        t_.matmul(out=pO[r][:, o0:o1],
                                                  lhsT=vtm[p][:, j * 128:(j + 1) * 128],
                                                  rhs=att[pa][:, 0:ce - cb],
                                                  start=st, stop=sp)
                                        return t_.matmul(out=pl[r][:, o0:o1],
                                                         lhsT=ones128[:],
                                                         rhs=att[pa][:, 0:ce - cb],
                                                         start=st, stop=sp)
                                    b.op("tensor", avmm,
                                         waits=[(s_ex, ext), (s_oe, oe_gate),
                                                (s_of, of_gate)],
                                         inc=s_av)
                            avt = b.tok(s_av)
                            def lnev(s):
                                s.activation(out=lntmp[:, 0:512], in_=pl[0][:],
                                             func=AF.Ln)
                                return s.activation(out=lntmp[:, 512:1024],
                                                    in_=pl[1][:], func=AF.Ln)
                            b.op("scalar", lnev, waits=[(s_av, avt)], inc=s_oe)
                            b.op("scalar", lambda s: s.activation(
                                out=rC[:], in_=lntmp[:], func=AF.Exp, scale=-1.0),
                                waits=[(s_oe, b.tok(s_oe))], inc=s_oe)
                            oet2 = b.tok(s_oe)
                            def omul(v, h=h, q0=q0):
                                v.tensor_tensor(
                                    out=o_all[h][:, q0:q0 + 512], in0=pO[0][:],
                                    in1=rC[:, 0:512], op=ALU.mult)
                                return v.tensor_tensor(
                                    out=o_all[h][:, q0 + 512:q0 + 1024],
                                    in0=pO[1][:], in1=rC[:, 512:1024],
                                    op=ALU.mult)
                            b.op("vector", omul,
                                 waits=[(s_oe, oet2), (s_av, avt)], inc=s_of)
            of_all = b.tok(s_of)

            with ExitStack() as a4s:
                p4 = lambda name, shape, dtp: a4s.enter_context(
                    nc.sbuf_tensor(tag + name, shape, dtp))
                wosb = [p4(f"wos{i}", [128, HN * 256], BF16) for i in range(2)]
                yosb = [p4(f"yos{i}", [128, 256], F32) for i in range(2)]
                borep = p4("borep", [128, D], BF16)
                b.op("gpsimd", lambda g: g.dma_start(out=borep[:], in_=dt["bo_r"]),
                     waits=[(s_of, of_all)], inc=s_wo + "b", inc_n=16)
                wob = (s_wo + "b", b.tok(s_wo + "b"))
                with (
                    nc.psum_tensor(tag + "pyo0", [128, 512], F32) as pyo0,
                    nc.psum_tensor(tag + "pyo1", [128, 512], F32) as pyo1,
                ):
                    pyo = [pyo0, pyo1]
                    for fb in range(16):
                        p = fb % 2
                        fsl = slice(fb * 256, (fb + 1) * 256)
                        b.op("gpsimd", lambda g, fb=fb, p=p: g.dma_start(
                            out=wosb[p][:].rearrange("p (j f) -> p j f", j=HN),
                            in_=dt["wo_t"][fb].rearrange("(j p) f -> p j f", p=128)),
                            waits=[(s_of, of_all), (s_yo, NT * (fb - 1))],
                            inc=s_wo + str(p), inc_n=16)
                        wot = (s_wo + str(p), b.tok(s_wo + str(p)))
                        for t in range(NT):
                            nzy = fb * NT + t
                            pp = t % 2
                            tsl = slice(t * 128, (t + 1) * 128)

                            def yomm(t_, p=p, pp=pp, tsl=tsl, fsl=fsl):
                                for k in range(HN):
                                    t_.matmul(out=pyo[pp][:, 0:256],
                                              lhsT=o_all[k][:, tsl],
                                              rhs=wosb[p][:, k * 256:(k + 1) * 256],
                                              start=(k == 0), stop=False)
                                return t_.matmul(out=pyo[pp][:, 0:256],
                                                 lhsT=e0_bf[:],
                                                 rhs=borep[:, fsl], start=False,
                                                 stop=True)
                            b.op("tensor", yomm,
                                 waits=[wot, wob, (s_of, of_all),
                                        (s_ye, nzy - 1)],
                                 inc=s_yo)
                            yot = b.tok(s_yo)
                            b.op("scalar", lambda s, pp=pp: s.activation(
                                out=yosb[pp][:], in_=pyo[pp][:, 0:256], func=AF.Identity),
                                waits=[(s_yo, yot),
                                       (s_acc + str(pp), 16 * (nzy // 2))],
                                inc=s_ye)
                            yet = b.tok(s_ye)
                            b.op("gpsimd", lambda g, tsl=tsl, fsl=fsl, pp=pp:
                                 g.dma_start(out=dst[tsl, fsl], in_=yosb[pp][:],
                                             accum_op=ALU.add),
                                 waits=[(s_ye, yet)], inc=s_acc + str(pp), inc_n=16)
        return b.abs_waits([(s_acc + "0", b.tok(s_acc + "0")),
                            (s_acc + "1", b.tok(s_acc + "1"))])

    if N_SUBLAYERS == 1:
        a1w = ffn_phase(dt["x"], dt["out"], "A", [])
        b.barrier()
        b.op("gpsimd", lambda g: None, waits=a1w)
    elif N_SUBLAYERS == 2:
        a1w = ffn_phase(dt["x"], dt["X1"], "A", [])
        b.barrier()
        a2w = attn_phase(dt["X1"], dt["X1"], "B", a1w)
        b.barrier()
        with nc.sbuf_tensor("cp", [128, D], F32) as cp:
            for t in range(NT):
                b.op("sync", lambda s, t=t: s.dma_start(
                    out=cp[:], in_=dt["X1"][t * 128:(t + 1) * 128, :]),
                    waits=a2w + [("cpw", 16 * t)], inc="cpr", inc_n=16)
                b.op("sync", lambda s, t=t: s.dma_start(
                    out=dt["out"][t * 128:(t + 1) * 128, :], in_=cp[:]),
                    waits=[("cpr", 16 * (t + 1))], inc="cpw", inc_n=16)
            b.op("sync", lambda s: None, waits=[("cpw", 16 * NT)])
    else:
        a1w = ffn_phase(dt["x"], dt["X1"], "A", [])
        b.barrier()
        a2w = attn_phase(dt["X1"], dt["X1"], "B", a1w)
        b.barrier()
        a3w = ffn_phase(dt["X1"], dt["out"], "C", a2w)
        b.barrier()
        b.op("gpsimd", lambda g: None, waits=a3w)

    b.emit()
    es.close()
    return nc


_NC_CACHE = None


def _get_nc():
    global _NC_CACHE
    if _NC_CACHE is None:
        _NC_CACHE = build_program()
    return _NC_CACHE


def stage_inputs(inputs):
    ii = {k: np.asarray(v) for k, v in inputs.items()}
    C = lambda a: np.ascontiguousarray(np.asarray(a, dtype=np.float32))

    wq_t = C(ii["L0_attn_Wq"].T.reshape(D, HN, HD).transpose(1, 0, 2))
    wk_t = C(ii["L0_attn_Wk"].T.reshape(D, HN, HD).transpose(1, 0, 2))
    wv_t = C(ii["L0_attn_Wv"].T.reshape(D, HN, HD).transpose(1, 0, 2))
    wo_t = C(ii["L0_attn_Wo"].T.reshape(D, 16, 256).transpose(1, 0, 2))
    bq = C(ii["L0_attn_bq"].reshape(HN, 128).T)
    bk = C(ii["L0_attn_bk"].reshape(HN, 128).T)
    bv = C(ii["L0_attn_bv"].reshape(HN, 128).T)
    bo_r = C(np.tile(np.asarray(ii["L0_attn_bo"], np.float32).reshape(1, D),
                     (128, 1)))
    w1_t = C(ii["L0_ffn_w1q"].T.reshape(1024, NM, 128).transpose(1, 0, 2))
    w3_t = C(ii["L0_ffn_w3q"].T.reshape(1024, NM, 128).transpose(1, 0, 2))
    w2_t = np.zeros((NK2 * 128, D), np.float32)
    w2_t[:2752] = np.asarray(ii["L0_ffn_w2q"], dtype=np.float32).T
    a1_t = np.zeros((D, 128), np.float32)
    a1_t[:, 0:2] = np.asarray(ii["L0_ffn_a1"], np.float32).T
    a3_t = np.zeros((D, 128), np.float32)
    a3_t[:, 0:2] = np.asarray(ii["L0_ffn_a3"], np.float32).T
    a2f = np.asarray(ii["L0_ffn_a2"], dtype=np.float32).T
    a2_t = np.zeros((88 * 128, 128), np.float32)
    for c in range(4):
        for kk in range(NK2):
            rows = 64 if kk == NK2 - 1 else 128
            r0 = 2752 * c + 128 * kk
            blk = c * NK2 + kk
            a2_t[blk * 128:blk * 128 + rows, 0:2] = a2f[r0:r0 + rows]
    b1_t = np.zeros((128, FF), np.float32)
    b1_t[0:2] = np.asarray(ii["L0_ffn_b1"], np.float32).T
    b3_t = np.zeros((128, FF), np.float32)
    b3_t[0:2] = np.asarray(ii["L0_ffn_b3"], np.float32).T
    b2_t = np.zeros((128, D), np.float32)
    b2_t[0:2] = np.asarray(ii["L0_ffn_b2"], np.float32).T

    return dict(wq_t=wq_t, wk_t=wk_t, wv_t=wv_t, wo_t=wo_t, bq=bq, bk=bk,
                bv=bv, bo_r=bo_r, w1_t=w1_t, w3_t=w3_t, w2_t=w2_t, a1_t=a1_t,
                a3_t=a3_t, a2_t=a2_t, b1_t=b1_t, b3_t=b3_t, b2_t=b2_t)


def kernel(**inputs):
    shared = stage_inputs(inputs)
    x = np.asarray(inputs["L0_attn_x"], dtype=np.float32)
    in_maps = [dict(shared, x=np.ascontiguousarray(x[i])) for i in range(8)]

    nc = _get_nc()
    res = run_bass_kernel_spmd(nc, in_maps, core_ids=list(range(8)))
    return np.stack([np.asarray(res.results[i]["out"]) for i in range(8)], axis=0)


if __name__ == "__main__":
    import reference as R
    ins = {k: np.asarray(v) for k, v in R.setup_inputs(0).items()}
    got = kernel(**ins)
    import jax
    import jax.numpy as jnp
    if N_SUBLAYERS == 3:
        want = np.asarray(R.reference(**ins))
    else:
        x = jnp.asarray(ins["L0_attn_x"])
        ln = (x - x.mean(-1, keepdims=True)) / jnp.sqrt(x.var(-1, keepdims=True) + 1e-5)

        def ffn_ref(xx):
            LORA = 0.5
            def ql(v, wq, A, Bm):
                w = jnp.tile(wq, (1, 4))
                return v @ w.T + (v @ A.T @ Bm.T) * LORA
            g = jax.nn.silu(ql(xx, jnp.asarray(ins["L0_ffn_w1q"]), jnp.asarray(ins["L0_ffn_a1"]), jnp.asarray(ins["L0_ffn_b1"])))
            u = ql(xx, jnp.asarray(ins["L0_ffn_w3q"]), jnp.asarray(ins["L0_ffn_a3"]), jnp.asarray(ins["L0_ffn_b3"]))
            return ql(g * u, jnp.asarray(ins["L0_ffn_w2q"]), jnp.asarray(ins["L0_ffn_a2"]), jnp.asarray(ins["L0_ffn_b2"]))
        x1 = x + 0.5 * ffn_ref(ln)
        if N_SUBLAYERS == 1:
            want = np.asarray(x1)
        else:
            ln1 = (x1 - x1.mean(-1, keepdims=True)) / jnp.sqrt(x1.var(-1, keepdims=True) + 1e-5)
            B_, L_, D_, H_ = 8, 2048, 4096, 32
            q = (ln1 @ jnp.asarray(ins["L0_attn_Wq"]).T + jnp.asarray(ins["L0_attn_bq"])).reshape(B_, L_, H_, 128).transpose(0, 2, 1, 3)
            k = (ln1 @ jnp.asarray(ins["L0_attn_Wk"]).T + jnp.asarray(ins["L0_attn_bk"])).reshape(B_, L_, H_, 128).transpose(0, 2, 1, 3)
            v = (ln1 @ jnp.asarray(ins["L0_attn_Wv"]).T + jnp.asarray(ins["L0_attn_bv"])).reshape(B_, L_, H_, 128).transpose(0, 2, 1, 3)
            sc = jnp.einsum("bhqd,bhkd->bhqk", q, k) / math.sqrt(128)
            causal = jnp.tril(jnp.ones((L_, L_), bool))
            sc = jnp.where(causal, sc, -jnp.inf)
            at = jax.nn.softmax(sc, axis=-1)
            o = jnp.einsum("bhqk,bhkd->bhqd", at, v).transpose(0, 2, 1, 3).reshape(B_, L_, D_)
            want = np.asarray(x1 + o @ jnp.asarray(ins["L0_attn_Wo"]).T + jnp.asarray(ins["L0_attn_bo"]))
    d = got - want
    rv = float((d ** 2).mean() / (want ** 2).mean())
    print("residual variance ratio:", rv)
    print("max abs err:", float(np.abs(d).max()))


# version 09
# speedup vs baseline: 1.1980x; 1.0946x over previous
import math
from contextlib import ExitStack

import numpy as np

import concourse.bass as bass
import concourse.mybir as mybir
from concourse.bass_utils import run_bass_kernel_spmd

F32 = mybir.dt.float32
BF16 = mybir.dt.bfloat16
AF = mybir.ActivationFunctionType
ALU = mybir.AluOpType
AX = mybir.AxisListType

T = 2048
D = 4096
HN = 32
HD = 128
FF = 11008
NT = T // 128
NJ = D // 128
NM = FF // 128
NK2 = 22
ISQ = 1.0 / math.sqrt(HD)
EPS = 1e-5
EXPB = -10.0

ENGINES = ("tensor", "scalar", "vector", "gpsimd", "sync")
N_SUBLAYERS = 3
FFN_STAGES = 5
F2_ALOAD = True
F3_GMM = True
F3_NM = NM
F3_LORA = True
SERIAL = 1
ZSER = 2
F3_SIL = True
F3_ZW = True
F2_UMM = True
F2_EVAC = True


class B:

    def __init__(self, nc, es):
        self.nc = nc
        self.es = es
        self.ops = {e: [] for e in ENGINES}
        self.sems = {}
        self.cnt = {}
        self.base = {}
        self.lastwait = {e: {} for e in ENGINES}
        self.nbar = 0

    def sem(self, name):
        if name not in self.sems:
            self.sems[name] = self.es.enter_context(self.nc.semaphore(name))
            self.cnt[name] = 0
        return self.sems[name]

    def new_phase(self):
        self.base = dict(self.cnt)

    def tok(self, name):
        return self.cnt.get(name, 0) - self.base.get(name, 0)

    def abs_waits(self, waits):
        return [(n, self.base.get(n, 0) + v, "abs") for (n, v) in waits]

    def op(self, eng, body, waits=(), inc=None, inc_n=1):
        wl = []
        for w in waits:
            if len(w) == 3:
                n, v, _ = w
            else:
                n, v = w
                if v is None or v <= 0:
                    continue
                v = self.base.get(n, 0) + v
            if v is None or v <= 0:
                continue
            if v > self.lastwait[eng].get(n, 0):
                self.lastwait[eng][n] = v
                wl.append((self.sem(n), v))
        it = None
        tok = None
        if inc is not None:
            s = self.sem(inc)
            self.cnt[inc] = self.cnt.get(inc, 0) + inc_n
            tok = self.cnt[inc] - self.base.get(inc, 0)
            it = (s, inc_n)

        def fn(E, wl=wl, body=body, it=it):
            for s, v in wl:
                E.wait_ge(s, v)
            inst = body(E)
            if it is not None:
                if inst is None:
                    inst = E.nop()
                inst.then_inc(*it)

        self.ops[eng].append(fn)
        return tok

    def barrier(self):
        self.nbar += 1
        target = 5 * self.nbar
        s = self.sem("bar")
        for e in ENGINES:
            def fn(E, s=s, target=target):
                E.nop().then_inc(s, 1)
                E.wait_ge(s, target)
            self.ops[e].append(fn)
            self.lastwait[e]["bar"] = target

    def emit(self):
        with self.nc.Block() as block:
            @block.tensor
            def _(E):
                for fn in self.ops["tensor"]:
                    fn(E)

            @block.scalar
            def _(E):
                for fn in self.ops["scalar"]:
                    fn(E)

            @block.vector
            def _(E):
                for fn in self.ops["vector"]:
                    fn(E)

            @block.gpsimd
            def _(E):
                for fn in self.ops["gpsimd"]:
                    fn(E)

            @block.sync
            def _(E):
                for fn in self.ops["sync"]:
                    fn(E)


def build_program():
    nc = bass.Bass("TRN2", target_bir_lowering=False, debug=False,
                   detect_race_conditions=False)
    es = ExitStack()
    b = B(nc, es)

    dt = {}
    dt["x"] = nc.dram_tensor("x", [T, D], F32, kind="ExternalInput").ap()
    dt["wq_t"] = nc.dram_tensor("wq_t", [HN, D, HD], F32, kind="ExternalInput").ap()
    dt["wk_t"] = nc.dram_tensor("wk_t", [HN, D, HD], F32, kind="ExternalInput").ap()
    dt["wv_t"] = nc.dram_tensor("wv_t", [HN, D, HD], F32, kind="ExternalInput").ap()
    dt["wo_t"] = nc.dram_tensor("wo_t", [16, D, 256], F32, kind="ExternalInput").ap()
    dt["bq"] = nc.dram_tensor("bq", [128, HN], F32, kind="ExternalInput").ap()
    dt["bk"] = nc.dram_tensor("bk", [128, HN], F32, kind="ExternalInput").ap()
    dt["bv"] = nc.dram_tensor("bv", [128, HN], F32, kind="ExternalInput").ap()
    dt["w1_t"] = nc.dram_tensor("w1_t", [NM, 1024, 128], F32, kind="ExternalInput").ap()
    dt["w3_t"] = nc.dram_tensor("w3_t", [NM, 1024, 128], F32, kind="ExternalInput").ap()
    dt["w2_t"] = nc.dram_tensor("w2_t", [NK2 * 128, D], F32, kind="ExternalInput").ap()
    dt["a1_t"] = nc.dram_tensor("a1_t", [D, 128], F32, kind="ExternalInput").ap()
    dt["a3_t"] = nc.dram_tensor("a3_t", [D, 128], F32, kind="ExternalInput").ap()
    dt["a2_t"] = nc.dram_tensor("a2_t", [88 * 128, 128], F32, kind="ExternalInput").ap()
    dt["b1_t"] = nc.dram_tensor("b1_t", [128, FF], F32, kind="ExternalInput").ap()
    dt["b3_t"] = nc.dram_tensor("b3_t", [128, FF], F32, kind="ExternalInput").ap()
    dt["b2_t"] = nc.dram_tensor("b2_t", [128, D], F32, kind="ExternalInput").ap()
    dt["bo_r"] = nc.dram_tensor("bo_r", [128, D], F32, kind="ExternalInput").ap()

    dt["out"] = nc.dram_tensor("out", [T, D], F32, kind="ExternalOutput").ap()

    dt["LN"] = nc.dram_tensor("LN", [T, D], BF16, kind="Internal").ap()
    dt["X1"] = nc.dram_tensor("X1", [T, D], F32, kind="Internal").ap()
    dt["Z"] = nc.dram_tensor("Z", [NM + 1, 128, T], BF16, kind="Internal").ap()
    dt["QS"] = nc.dram_tensor("QS", [HN, 128, T], BF16, kind="Internal").ap()
    dt["KS"] = nc.dram_tensor("KS", [HN, 128, T], BF16, kind="Internal").ap()
    dt["VS"] = nc.dram_tensor("VS", [HN, 128, T], BF16, kind="Internal").ap()

    sb = lambda name, shape, dtp: es.enter_context(nc.sbuf_tensor(name, shape, dtp))

    it_f = sb("it_f", [128, 128], F32)
    ident = sb("ident", [128, 128], BF16)
    maskT = sb("maskT", [128, 128], F32)
    ones128 = sb("ones128", [128, 128], BF16)
    ident64 = sb("ident64", [128, 128], BF16)
    e0_bf = sb("e0_bf", [128, 128], BF16)
    bqs_sb = sb("bqs_sb", [128, HN], F32)
    bk_sb = sb("bk_sb", [128, HN], F32)
    bv_sb = sb("bv_sb", [128, HN], F32)
    u1_bf = sb("u1_bf", [128, T], BF16)
    u3_bf = sb("u3_bf", [128, T], BF16)
    u2_bf = sb("u2_bf", [128, T], BF16)
    eps_t = sb("eps_t", [128, 1], F32)
    expb_t = sb("expb_t", [128, 1], F32)
    zzero = sb("zzero", [128, T], BF16)

    b.op("gpsimd", lambda g: g.iota(it_f[:], [[1, 128]], channel_multiplier=-1,
                                    allow_small_or_imprecise_dtypes=True), inc="p0g")
    t0 = b.tok("p0g")

    def mk_consts(v):
        v.tensor_single_scalar(out=maskT[:], in_=it_f[:], scalar=0.0, op=ALU.is_lt)
        v.tensor_scalar_mul(out=maskT[:], in0=maskT[:], scalar1=-1e30)
        v.tensor_single_scalar(out=it_f[:], in_=it_f[:], scalar=0.0, op=ALU.is_equal)
        v.tensor_copy(out=ident[:], in_=it_f[:])
        v.memset(ident64[:], 0.0)
        v.tensor_copy(out=ident64[0:64, :], in_=ident[0:64, :])
        v.memset(ones128[:], 1.0)
        v.memset(eps_t[:], EPS)
        v.memset(expb_t[:], EXPB)
        v.memset(u1_bf[:], 0.0)
        v.memset(u3_bf[:], 0.0)
        v.memset(zzero[:], 0.0)
        return v.memset(u2_bf[:], 0.0)
    b.op("vector", mk_consts, waits=[("p0g", t0)], inc="p0v")
    b.op("gpsimd", lambda g: g.iota(it_f[:], [[0, 128]], channel_multiplier=1,
                                    allow_small_or_imprecise_dtypes=True),
         waits=[("p0v", 1)], inc="p0g")
    t0b = b.tok("p0g")
    def mk_e0(v):
        v.tensor_single_scalar(out=it_f[:], in_=it_f[:], scalar=0.0, op=ALU.is_equal)
        return v.tensor_copy(out=e0_bf[:], in_=it_f[:])
    b.op("vector", mk_e0, waits=[("p0g", t0b)], inc="p0v")

    b.op("sync", lambda s: s.dma_start(out=bqs_sb[:], in_=dt["bq"]), inc="p0e", inc_n=16)
    b.op("sync", lambda s: s.dma_start(out=bk_sb[:], in_=dt["bk"]), inc="p0e", inc_n=16)
    b.op("sync", lambda s: s.dma_start(out=bv_sb[:], in_=dt["bv"]), inc="p0e", inc_n=16)
    t1 = b.tok("p0e")
    b.op("vector", lambda v: v.tensor_scalar_mul(out=bqs_sb[:], in0=bqs_sb[:], scalar1=ISQ),
         waits=[("p0e", t1)], inc="p0v")
    b.barrier()

    def ln_pass(src, tag, ph, ready_fn):
        s_x, s_st, s_rs, s_ln = "Lx", "Lst", "Lrs", "Lln"
        s_lnw, s_sq2 = "Llnw", "Lsq2"
        psb = lambda name, shape, dtp: ph.enter_context(
            nc.sbuf_tensor(tag + name, shape, dtp))
        xt = [psb(f"xt{i}", [128, D], F32) for i in range(4)]
        sq = psb("sq", [128, D], F32)
        lnt = [psb(f"lnt{i}", [128, D], BF16) for i in range(2)]
        stm = [psb(f"stm{i}", [128, 1], F32) for i in range(2)]
        stv = [psb(f"stv{i}", [128, 1], F32) for i in range(2)]
        str_ = [psb(f"str{i}", [128, 1], F32) for i in range(2)]

        def emit_load(tl):
            p4 = tl % 4
            w = [(s_ln, tl - 3)]
            b.op("sync" if tl % 2 == 0 else "scalar",
                 lambda s, tl=tl, p4=p4: s.dma_start(
                     out=xt[p4][:], in_=src[tl * 128:(tl + 1) * 128, :]),
                 waits=list(ready_fn(tl)) + w, inc=s_x + str(tl % 2), inc_n=16)
            return (s_x + str(tl % 2), b.tok(s_x + str(tl % 2)))

        def emit_back(t, p, p4, rst):
            b.op("vector", lambda v, p=p, p4=p4: v.tensor_scalar(
                out=lnt[p][:], in0=xt[p4][:], scalar1=stm[p][:],
                scalar2=str_[p][:], op0=ALU.subtract, op1=ALU.mult),
                waits=[(s_rs, rst), (s_lnw + str(p), 16 * (t // 2))],
                inc=s_ln)
            b.op("sync" if p == 0 else "scalar",
                 lambda s, t=t, p=p: s.dma_start(
                     out=dt["LN"][t * 128:(t + 1) * 128, :], in_=lnt[p][:]),
                 waits=[(s_ln, b.tok(s_ln))], inc=s_lnw + str(p), inc_n=16)

        xls = [emit_load(0), emit_load(1)]
        pend = []
        for t in range(NT):
            p = t % 2
            p4 = t % 4
            if t + 2 < NT:
                xls.append(emit_load(t + 2))
            xl = xls[t]

            b.op("scalar", lambda s, p=p, p4=p4: s.activation(
                out=sq[:], in_=xt[p4][:], func=AF.Square,
                accum_out=stv[p][:]),
                waits=[xl, (s_st, 3 * (t - 1))], inc=s_sq2)
            sq2t = b.tok(s_sq2)
            b.op("vector", lambda v, p=p, p4=p4: v.reduce_sum(
                out=stm[p][:], in_=xt[p4][:], axis=AX.X),
                waits=[xl], inc=s_st)
            def stats2(v, p=p):
                v.tensor_scalar_mul(out=stm[p][:], in0=stm[p][:], scalar1=1.0 / D)
                return v.tensor_scalar_mul(out=stv[p][:], in0=stv[p][:],
                                           scalar1=1.0 / D)
            b.op("vector", stats2, waits=[(s_st, b.tok(s_st)), (s_sq2, sq2t)],
                 inc=s_st)
            b.op("vector", lambda v, p=p: v.scalar_tensor_tensor(
                out=stv[p][:], in0=stm[p][:], scalar=stm[p][:], in1=stv[p][:],
                op0=ALU.mult, op1=ALU.subtract),
                waits=[(s_st, b.tok(s_st))], inc=s_st)

            b.op("scalar", lambda s, p=p: s.activation(
                out=str_[p][:], in_=stv[p][:], func=AF.Ln,
                scale=-1.0, bias=eps_t[:]),
                waits=[(s_st, b.tok(s_st))], inc=s_rs)
            b.op("scalar", lambda s, p=p: s.activation(
                out=str_[p][:], in_=str_[p][:], func=AF.Exp, scale=-0.5),
                waits=[(s_rs, b.tok(s_rs))], inc=s_rs)
            pend.append((t, p, p4, b.tok(s_rs)))
            if len(pend) > 1:
                emit_back(*pend.pop(0))
        while pend:
            emit_back(*pend.pop(0))
        lnw_w = [(s_lnw + "0", b.tok(s_lnw + "0")), (s_lnw + "1", b.tok(s_lnw + "1"))]
        return lnw_w

    def ffn_phase(src, dst, tag, ready_fn):
        b.new_phase()
        s_lnr = "Flnr"
        s_u = "Fu"
        s_ue = "Fue"
        s_xs = "Fxs"
        s_xc = "Fxc"
        s_w = "Fw"
        s_pw = "Fpw"
        s_g = "Fg"
        s_si = "Fsi"
        s_z = "Fz"
        s_zw = "Fzw"
        s_zr = "Fzr"
        s_zs = "Fzs"
        s_zc = "Fzc"
        s_u2 = "Fu2"
        s_u2e = "Fu2e"
        s_w2 = "Fw2"
        s_y = "Fy"
        s_ye = "Fye"
        s_acc = "Facc"

        with ExitStack() as ph:
            psb = lambda name, shape, dtp: ph.enter_context(
                nc.sbuf_tensor(tag + name, shape, dtp))

            swa = ExitStack()
            pwa = lambda name, shape, dtp: swa.enter_context(
                nc.sbuf_tensor(tag + name, shape, dtp))
            a1t_sb = pwa("a1t", [128, HN * 128], BF16)
            a3t_sb = pwa("a3t", [128, HN * 128], BF16)
            b1sb = pwa("b1s", [128, FF], BF16)
            b3sb = pwa("b3s", [128, FF], BF16)
            if F2_ALOAD:
                b.op("gpsimd", lambda g: g.dma_start(
                    out=a1t_sb[:].rearrange("p (j f) -> p j f", j=HN),
                    in_=dt["a1_t"].rearrange("(j p) f -> p j f", p=128)),
                    waits=list(ready_fn(None)), inc="Faw", inc_n=16)
                b.op("gpsimd", lambda g: g.dma_start(
                    out=a3t_sb[:].rearrange("p (j f) -> p j f", j=HN),
                    in_=dt["a3_t"].rearrange("(j p) f -> p j f", p=128)),
                    inc="Faw", inc_n=16)
            aw_all = b.tok("Faw")
            b.op("gpsimd", lambda g: g.dma_start(out=b1sb[:], in_=dt["b1_t"]),
                 waits=list(ready_fn(None)), inc="Fwb", inc_n=16)
            b.op("gpsimd", lambda g: g.dma_start(out=b3sb[:], in_=dt["b3_t"]),
                 inc="Fwb", inc_n=16)
            wb = ("Fwb", b.tok("Fwb"))

            with ExitStack() as s1:
                lnw_w = ln_pass(src, tag + "a", s1, ready_fn)

            sx = ExitStack()
            xs_bf = [sx.enter_context(nc.sbuf_tensor(tag + f"xsb{i}", [128, T], BF16))
                     for i in range(8)]

            with ExitStack() as s2:
                p2 = lambda name, shape, dtp: s2.enter_context(
                    nc.sbuf_tensor(tag + name, shape, dtp))
                lnf = [p2(f"lnf{i}", [128, T], BF16) for i in range(3)]
                xs_acc = [p2(f"xsa{i}", [128, T], F32) for i in range(8)]
                with ExitStack() as pst:
                    pu1 = [pst.enter_context(nc.psum_tensor(tag + f"pu1_{i}",
                           [128, 512], F32)) for i in range(4)]
                    pu3 = [pst.enter_context(nc.psum_tensor(tag + f"pu3_{i}",
                           [128, 512], F32)) for i in range(4)]
                    for j in range(NJ):
                        p = j % 3
                        b.op("sync", lambda s, j=j, p=p: s.dma_start_transpose(
                            out=lnf[p][:], in_=dt["LN"][:, j * 128:(j + 1) * 128]),
                            waits=lnw_w + [(s_xs, j - 2)] +
                                  ([(s_u, 8 * (j - 2))] if F2_UMM else []),
                            inc=s_lnr + str(p), inc_n=16)
                        lr = (s_lnr + str(p), b.tok(s_lnr + str(p)))

                        def umm(t_, j=j, p=p):
                            last = None
                            for fb in range(4):
                                t_.matmul(out=pu1[fb][:],
                                          lhsT=a1t_sb[:, j * 128:(j + 1) * 128],
                                          rhs=lnf[p][:, fb * 512:(fb + 1) * 512],
                                          start=(j == 0), stop=(j == NJ - 1))
                            for fb in range(4):
                                last = t_.matmul(out=pu3[fb][:],
                                          lhsT=a3t_sb[:, j * 128:(j + 1) * 128],
                                          rhs=lnf[p][:, fb * 512:(fb + 1) * 512],
                                          start=(j == 0), stop=(j == NJ - 1))
                            return last
                        if F2_UMM:
                            b.op("tensor", umm,
                                 waits=[lr, ("Faw", aw_all)],
                                 inc=s_u, inc_n=8)
                        jp = j % 8
                        if j < 8:
                            b.op("vector", lambda v, jp=jp, p=p: v.tensor_copy(
                                out=xs_acc[jp][:], in_=lnf[p][:]),
                                waits=[lr], inc=s_xs)
                        else:
                            b.op("vector", lambda v, jp=jp, p=p: v.tensor_tensor(
                                out=xs_acc[jp][:], in0=xs_acc[jp][:], in1=lnf[p][:],
                                op=ALU.add),
                                waits=[lr], inc=s_xs)
                    xs_all = b.tok(s_xs)
                    for jp in range(8):
                        b.op("vector", lambda v, jp=jp: v.tensor_copy(
                            out=xs_bf[jp][:], in_=xs_acc[jp][:]),
                            waits=[(s_xs, xs_all)], inc=s_xc)
                    u_all = b.tok(s_u)
                    if F2_EVAC:
                        def uev(s):
                            last = None
                            for fb in range(4):
                                s.activation(out=u1_bf[:, fb * 512:(fb + 1) * 512],
                                             in_=pu1[fb][:], func=AF.Identity,
                                             scale=0.5)
                                last = s.activation(
                                    out=u3_bf[:, fb * 512:(fb + 1) * 512],
                                    in_=pu3[fb][:], func=AF.Identity, scale=0.5)
                            return last
                        b.op("scalar", uev, waits=[(s_u, u_all)], inc=s_ue)
                        b.op("scalar", lambda s: None,
                             waits=[(s_ue, b.tok(s_ue))], inc=s_ue)
            ue_all = b.tok(s_ue)
            xc_all = b.tok(s_xc)


            with ExitStack() as s3:
                p3 = lambda name, shape, dtp: s3.enter_context(
                    nc.sbuf_tensor(tag + name, shape, dtp))
                w1sb = [p3(f"w1s{i}", [128, 1024], BF16) for i in range(2)]
                w3sb = [p3(f"w3s{i}", [128, 1024], BF16) for i in range(2)]
                sil = [p3(f"sil{i}", [128, 512], F32) for i in range(4)]
                zst = [p3(f"zst{i}", [128, T], BF16) for i in range(2)]
                gate2 = [(s_xc, xc_all), (s_ue, ue_all)]
                with ExitStack() as pgs:
                    pgt = [pgs.enter_context(nc.psum_tensor(tag + f"pg{i}",
                           [128, 512], F32)) for i in range(8)]
                    pg = [(pgt[0], pgt[1]), (pgt[2], pgt[3]),
                          (pgt[4], pgt[5]), (pgt[6], pgt[7])]
                    for m in range(F3_NM):
                        p = m % 2
                        b.op("gpsimd", lambda g, m=m, p=p: g.dma_start(
                            out=w1sb[p][:].rearrange("p (k f) -> p k f", k=8),
                            in_=dt["w1_t"][m].rearrange("(k p) f -> p k f", p=128)),
                            waits=[(s_pw, m - 1)] + (gate2 if m < 2 else []),
                            inc=s_w + str(p), inc_n=16)
                        b.op("gpsimd", lambda g, m=m, p=p: g.dma_start(
                            out=w3sb[p][:].rearrange("p (k f) -> p k f", k=8),
                            in_=dt["w3_t"][m].rearrange("(k p) f -> p k f", p=128)),
                            inc=s_w + str(p), inc_n=16)
                        wt = (s_w + str(p), b.tok(s_w + str(p)))
                        for tq in range(4):
                            nz = 4 * m + tq
                            pp = nz % 4
                            g1p, g3p = pg[pp]
                            sl = slice(tq * 512, (tq + 1) * 512)
                            if not F3_GMM:
                                continue

                            def gmm(t_, m=m, p=p, sl=sl, g1p=g1p, g3p=g3p):
                                for k in range(8):
                                    t_.matmul(out=g1p[:],
                                              lhsT=w1sb[p][:, k * 128:(k + 1) * 128],
                                              rhs=xs_bf[k][:, sl],
                                              start=(k == 0), stop=not (F3_LORA or k < 7))
                                if F3_LORA:
                                    t_.matmul(out=g1p[:],
                                              lhsT=b1sb[:, m * 128:(m + 1) * 128],
                                              rhs=u1_bf[:, sl], start=False, stop=True)
                                for k in range(8):
                                    last = t_.matmul(out=g3p[:],
                                              lhsT=w3sb[p][:, k * 128:(k + 1) * 128],
                                              rhs=xs_bf[k][:, sl],
                                              start=(k == 0), stop=not (F3_LORA or k < 7))
                                if F3_LORA:
                                    last = t_.matmul(out=g3p[:],
                                          lhsT=b3sb[:, m * 128:(m + 1) * 128],
                                          rhs=u3_bf[:, sl], start=False, stop=True)
                                return last
                            b.op("tensor", gmm,
                                 waits=[wt, wb, (s_xc, xc_all), (s_ue, ue_all)] +
                                       ([(s_z, nz - 3)] if F3_SIL else
                                        [(s_g, nz - 1)]),
                                 inc=s_g)
                            gt = b.tok(s_g)
                            if not F3_SIL:
                                continue
                            b.op("scalar", lambda s, pp=pp, g1p=g1p: s.activation(
                                out=sil[pp][:], in_=g1p[:], func=AF.Silu),
                                waits=[(s_g, gt), (s_z, nz - 3)], inc=s_si)
                            sit = b.tok(s_si)
                            b.op("vector", lambda v, pp=pp, p=p, sl=sl, g3p=g3p:
                                 v.tensor_tensor(out=zst[p][:, sl], in0=sil[pp][:],
                                                 in1=g3p[:], op=ALU.mult),
                                 waits=[(s_si, sit)] +
                                       ([(s_zw + str(p), 16 * (m // 2))]
                                        if F3_ZW else []),
                                 inc=s_z)
                        zt = b.tok(s_z)
                        b.op("tensor", lambda t_: None,
                             waits=[(s_g, b.tok(s_g))], inc=s_pw)
                        if F3_ZW and F3_SIL:
                            b.op("sync", lambda s, m=m, p=p: s.dma_start(
                                out=dt["Z"][m], in_=zst[p][:]),
                                waits=[(s_z, zt)], inc=s_zw + str(p), inc_n=16)
                zw_w = [(s_zw + "0", b.tok(s_zw + "0")),
                        (s_zw + "1", b.tok(s_zw + "1"))]
            sx.close()
            swa.close()

            zs_bf = [psb(f"zsb{i}", [128, T], BF16) for i in range(NK2)]
            zflat = dt["Z"].rearrange("m p t -> (m p) t")
            b.op("sync", lambda s: s.dma_start(out=dt["Z"][NM], in_=zzero[:]),
                 waits=[("p0v", 3, "abs")], inc="Fzp", inc_n=16)
            fzp = ("Fzp", b.tok("Fzp"))
            with ExitStack() as s45:
                p45 = lambda name, shape, dtp: s45.enter_context(
                    nc.sbuf_tensor(tag + name, shape, dtp))
                a2t_sb = p45("a2t", [128, 88 * 128], BF16)
                w2sb = [p45(f"w2s{i}", [128, NK2 * 512], BF16) for i in range(2)]
                b2sb = p45("b2s", [128, D], BF16)
                szld = ExitStack()
                zld = [szld.enter_context(nc.sbuf_tensor(
                    tag + f"zld{i}", [128, T], BF16)) for i in range(5)]
                b.op("gpsimd", lambda g: g.dma_start(
                    out=a2t_sb[:].rearrange("p (m f) -> p m f", m=88),
                    in_=dt["a2_t"].rearrange("(m p) f -> p m f", p=128)),
                    waits=zw_w + [fzp], inc="Fa2w", inc_n=16)
                a2w_all = b.tok("Fa2w")
                b.op("gpsimd", lambda g: g.dma_start(out=b2sb[:], in_=dt["b2_t"]),
                     waits=zw_w + [fzp], inc=s_w2 + "b", inc_n=16)
                w2b = (s_w2 + "b", b.tok(s_w2 + "b"))
                with (
                    nc.psum_tensor(tag + "pu2q0", [128, 512], F32) as pu2q0,
                    nc.psum_tensor(tag + "pu2q1", [128, 512], F32) as pu2q1,
                    nc.psum_tensor(tag + "pu2q2", [128, 512], F32) as pu2q2,
                    nc.psum_tensor(tag + "pu2q3", [128, 512], F32) as pu2q3,
                    nc.psum_tensor(tag + "zp0", [128, 512], F32) as zp0,
                    nc.psum_tensor(tag + "zp1", [128, 512], F32) as zp1,
                    nc.psum_tensor(tag + "py0", [128, 512], F32) as py0,
                    nc.psum_tensor(tag + "py1", [128, 512], F32) as py1,
                ):
                    pu2q = [pu2q0, pu2q1, pu2q2, pu2q3]
                    zp = [zp0, zp1]
                    py = [py0, py1]
                    zrs = []
                    for kk in range(NK2):
                        for c in range(4):
                            idx = kk * 4 + c
                            p = idx % 5
                            r0 = 2752 * c + 128 * kk
                            m0 = r0 // 128
                            m1 = (r0 + 127) // 128
                            lw = []
                            for mm_ in {m0, m1}:
                                if mm_ >= NM:
                                    lw.append(fzp)
                                else:
                                    lw.append((s_zw + str(mm_ % 2),
                                               16 * (mm_ // 2 + 1)))
                            if idx >= 5:
                                lw.append((s_u2, idx - 4))
                                lw.append(("Fzm", (idx - 5) // 4 + 1))
                            b.op("sync", lambda s, p=p, r0=r0:
                                 s.dma_start(out=zld[p][:],
                                             in_=zflat[r0:r0 + 128, :]),
                                 waits=lw, inc=s_zr + str(p), inc_n=16)
                            zr = (s_zr + str(p), b.tok(s_zr + str(p)))
                            zrs.append(zr)
                            blk = c * NK2 + kk
                            rows = 64 if kk == NK2 - 1 else 128

                            def u2mm(t_, p=p, blk=blk, rows=rows, idx=idx):
                                last = None
                                for q in range(4):
                                    last = t_.matmul(
                                        out=pu2q[q][:],
                                        lhsT=a2t_sb[0:rows,
                                                    blk * 128:(blk + 1) * 128],
                                        rhs=zld[p][0:rows,
                                                   q * 512:(q + 1) * 512],
                                        start=(idx == 0), stop=(idx == 87))
                                return last
                            b.op("tensor", u2mm,
                                 waits=[zr, ("Fa2w", a2w_all)], inc=s_u2)
                        idn = ident64 if kk == NK2 - 1 else ident
                        for fb in range(4):
                            ze = kk * 4 + fb
                            pz = ze % 2

                            def zsmm(t_, kk=kk, fb=fb, pz=pz, idn=idn):
                                last = None
                                for c in range(4):
                                    pbuf = (kk * 4 + c) % 5
                                    last = t_.matmul(
                                        out=zp[pz][:],
                                        lhsT=idn[:],
                                        rhs=zld[pbuf][:, fb * 512:(fb + 1) * 512],
                                        start=(c == 0), stop=(c == 3))
                                return last
                            b.op("tensor", zsmm,
                                 waits=zrs[kk * 4:kk * 4 + 4] +
                                       [(s_zc, ze - 1)],
                                 inc="Fzm" if fb == 3 else "Fzmq")
                            b.op("scalar", lambda s, kk=kk, fb=fb, pz=pz:
                                 s.activation(
                                     out=zs_bf[kk][:, fb * 512:(fb + 1) * 512],
                                     in_=zp[pz][:], func=AF.Identity),
                                 waits=[("Fzm" if fb == 3 else "Fzmq",
                                         b.tok("Fzm" if fb == 3 else "Fzmq"))],
                                 inc=s_zc)
                    u2t = b.tok(s_u2)
                    def u2ev(s):
                        last = None
                        for q in range(4):
                            last = s.activation(
                                out=u2_bf[:, q * 512:(q + 1) * 512],
                                in_=pu2q[q][:], func=AF.Identity, scale=0.5)
                        return last
                    b.op("scalar", u2ev, waits=[(s_u2, u2t)], inc=s_u2e)
                    zc_all = b.tok(s_zc)
                    u2e_all = b.tok(s_u2e)
                    szld.close()
                    yrb = [p45(f"yrb{i}", [128, 512], F32) for i in range(3)]
                    xsr = [p45(f"xsr{i}", [128, 512], F32) for i in range(3)]

                    for fb in range(8):
                        p = fb % 2
                        fsl = slice(fb * 512, (fb + 1) * 512)
                        b.op("gpsimd", lambda g, p=p, fsl=fsl: g.dma_start(
                            out=w2sb[p][:].rearrange("p (k f) -> p k f", k=NK2),
                            in_=dt["w2_t"][:, fsl].rearrange("(k p) f -> p k f", p=128)),
                            waits=zw_w + [fzp, (s_y, NT * (fb - 1))],
                            inc=s_w2 + str(p), inc_n=16)
                        w2t_ = (s_w2 + str(p), b.tok(s_w2 + str(p)))
                        for t in range(NT):
                            nzy = fb * NT + t
                            pp = t % 2
                            c3 = nzy % 3
                            tsl = slice(t * 128, (t + 1) * 128)
                            b.op("sync", lambda s, c3=c3, tsl=tsl, fsl=fsl:
                                 s.dma_start(out=xsr[c3][:], in_=src[tsl, fsl]),
                                 waits=[(s_ye, nzy - 2)] +
                                       ([(s_zc, zc_all), (s_u2e, u2e_all)] +
                                        list(ready_fn(None)) if nzy < 3 else []),
                                 inc="Fsr", inc_n=16)
                            srt = b.tok("Fsr")

                            def ymm(t_, p=p, pp=pp, tsl=tsl, fb=fb):
                                for k in range(NK2):
                                    t_.matmul(
                                        out=py[pp][:],
                                        lhsT=zs_bf[k][:, tsl],
                                        rhs=w2sb[p][:, k * 512:(k + 1) * 512],
                                        start=(k == 0), stop=False)
                                return t_.matmul(
                                    out=py[pp][:],
                                    lhsT=u2_bf[:, tsl],
                                    rhs=b2sb[:, fb * 512:(fb + 1) * 512],
                                    start=False, stop=True)
                            b.op("tensor", ymm,
                                 waits=[w2t_, w2b, (s_zc, zc_all),
                                        (s_u2e, u2e_all), (s_ye, nzy - 1)],
                                 inc=s_y)
                            yt = b.tok(s_y)
                            b.op("vector", lambda v, pp=pp, c3=c3:
                                 v.scalar_tensor_tensor(
                                     out=yrb[c3][:], in0=py[pp][:], scalar=0.5,
                                     in1=xsr[c3][:], op0=ALU.mult, op1=ALU.add),
                                 waits=[(s_y, yt), ("Fsr", srt),
                                        ("Fwr", 16 * (nzy - 2))],
                                 inc=s_ye)
                            yet = b.tok(s_ye)
                            b.op("scalar", lambda s, tsl=tsl, fsl=fsl, c3=c3:
                                 s.dma_start(out=dst[tsl, fsl], in_=yrb[c3][:]),
                                 waits=[(s_ye, yet)], inc="Fwr", inc_n=16)
        tile_w = {t: b.abs_waits([("Fwr", 16 * (112 + t + 1))])
                  for t in range(NT)}
        full_w = b.abs_waits([("Fwr", 16 * 128)])
        return lambda t: full_w if t is None else tile_w[t]

    def attn_phase(src, dst, tag, ready_fn):
        b.new_phase()
        s_lnr = "Blnr"
        s_w = "Fw"
        s_p = "Bp"
        s_pe = "Bpe"
        s_qw = "Bqw"
        s_hl = "Bhl"
        s_vt = "Bvt"
        s_ve = "Bve"
        s_stm = "Bsm"
        s_msk = "Bmk"
        s_ex = "Bex"
        s_av = "Bav"
        s_oe = "Boe"
        s_rc = "Brc"
        s_cb = "Bcb"
        s_of = "Bof"
        s_wo = "Bwo"
        s_yo = "Fy"
        s_ye = "Fye"
        s_acc = "Facc"

        with ExitStack() as ph:
            psb = lambda name, shape, dtp: ph.enter_context(
                nc.sbuf_tensor(tag + name, shape, dtp))

            with ExitStack() as a1s:
                lnw_w = ln_pass(src, tag + "a", a1s, ready_fn)

            with ExitStack() as a2s:
                p2 = lambda name, shape, dtp: a2s.enter_context(
                    nc.sbuf_tensor(tag + name, shape, dtp))
                ln2 = [p2(f"ln2_{j}", [128, T], BF16) for j in range(NJ)]
                wsb = [p2(f"wsb{i}", [128, D], BF16) for i in range(3)]
                qkv_sb = [p2(f"qkv{i}", [128, T], BF16) for i in range(2)]
                for j in range(NJ):
                    b.op("sync", lambda s, j=j: s.dma_start_transpose(
                        out=ln2[j][:], in_=dt["LN"][:, j * 128:(j + 1) * 128]),
                        waits=lnw_w, inc=s_lnr, inc_n=16)
                lnr_all = b.tok(s_lnr)
                projs = [(dt["wq_t"], dt["QS"], bqs_sb, ISQ),
                         (dt["wk_t"], dt["KS"], bk_sb, 1.0),
                         (dt["wv_t"], dt["VS"], bv_sb, 1.0)]
                with ExitStack() as pst:
                    pq = [[pst.enter_context(nc.psum_tensor(tag + f"pq{i}_{fb}",
                           [128, 512], F32)) for fb in range(4)] for i in range(2)]
                    nproj = 0
                    for h in range(HN):
                        for (wsrc, dsts, bias, scl) in projs:
                            wp = nproj % 3
                            pp = nproj % 2
                            b.op("gpsimd", lambda g, h=h, wsrc=wsrc, wp=wp:
                                 g.dma_start(
                                     out=wsb[wp][:].rearrange("p (j f) -> p j f", j=NJ),
                                     in_=wsrc[h].rearrange("(j p) f -> p j f", p=128)),
                                 waits=lnw_w + [(s_p, nproj - 2)],
                                 inc=s_w + str(wp), inc_n=16)
                            wt = (s_w + str(wp), b.tok(s_w + str(wp)))

                            def pmm(t_, wp=wp, pp=pp):
                                last = None
                                for j in range(NJ):
                                    for fb in range(4):
                                        last = t_.matmul(
                                            out=pq[pp][fb][:],
                                            lhsT=wsb[wp][:, j * 128:(j + 1) * 128],
                                            rhs=ln2[j][:, fb * 512:(fb + 1) * 512],
                                            start=(j == 0), stop=(j == NJ - 1))
                                return last
                            b.op("tensor", pmm,
                                 waits=[wt, (s_lnr, lnr_all),
                                        (s_pe, nproj - 1)],
                                 inc=s_p)
                            pt = b.tok(s_p)
                            def qev(s, pp=pp, bias=bias, scl=scl, h=h):
                                last = None
                                for fb in range(4):
                                    last = s.activation(
                                        out=qkv_sb[pp][:, fb * 512:(fb + 1) * 512],
                                        in_=pq[pp][fb][:], func=AF.Identity,
                                        bias=bias[:, h:h + 1], scale=scl)
                                return last
                            b.op("scalar", qev,
                                 waits=[(s_p, pt),
                                        (s_qw + str(pp), 16 * (nproj // 2))],
                                 inc=s_pe)
                            pet = b.tok(s_pe)
                            b.op("sync", lambda s, dsts=dsts, h=h, pp=pp:
                                 s.dma_start(out=dsts[h], in_=qkv_sb[pp][:]),
                                 waits=[(s_pe, pet)], inc=s_qw + str(pp), inc_n=16)
                            nproj += 1
            qw_w = [(s_qw + "0", b.tok(s_qw + "0")), (s_qw + "1", b.tok(s_qw + "1"))]

            o_all = [psb(f"oall{k}", [128, T], BF16) for k in range(HN)]
            with ExitStack() as a3s:
                p3 = lambda name, shape, dtp: a3s.enter_context(
                    nc.sbuf_tensor(tag + name, shape, dtp))
                qh = [p3(f"qh{i}", [128, T], BF16) for i in range(2)]
                kh = [p3(f"kh{i}", [128, T], BF16) for i in range(2)]
                vh = [p3(f"vh{i}", [128, T], BF16) for i in range(2)]
                vtm = [p3(f"vtm{i}", [128, T], BF16) for i in range(2)]
                att = [p3(f"att{i}", [128, 512], BF16) for i in range(4)]
                lntmp = p3("lntmp", [128, 1024], F32)
                rC = p3("rC", [128, 1024], F32)
                with (
                    nc.psum_tensor(tag + "ps0", [128, 512], F32) as ps0,
                    nc.psum_tensor(tag + "ps1", [128, 512], F32) as ps1,
                    nc.psum_tensor(tag + "ps2", [128, 512], F32) as ps2,
                    nc.psum_tensor(tag + "ps3", [128, 512], F32) as ps3,
                    nc.psum_tensor(tag + "pO0", [128, 512], F32) as pO0,
                    nc.psum_tensor(tag + "pO1", [128, 512], F32) as pO1,
                    nc.psum_tensor(tag + "pl0", [128, 512], F32) as pl0,
                    nc.psum_tensor(tag + "pl1", [128, 512], F32) as pl1,
                ):
                    pO = [pO0, pO1]
                    pl = [pl0, pl1]
                    ps = [ps0, ps1, ps2, ps3]
                    nblk = 0
                    for h in range(HN):
                        p = h % 2
                        b.op("sync", lambda s, h=h, p=p: s.dma_start(
                            out=qh[p][:], in_=dt["QS"][h]),
                            waits=qw_w +
                                  ([(s_of, 2 * (h - 1))] if h >= 2 else []),
                            inc=s_hl + str(p), inc_n=16)
                        b.op("sync", lambda s, h=h, p=p: s.dma_start(
                            out=kh[p][:], in_=dt["KS"][h]), inc=s_hl + str(p), inc_n=16)
                        b.op("sync", lambda s, h=h, p=p: s.dma_start(
                            out=vh[p][:], in_=dt["VS"][h]), inc=s_hl + str(p), inc_n=16)
                        hl = (s_hl + str(p), b.tok(s_hl + str(p)))
                        exp_prev = b.tok(s_ex)
                        for tt in range(NT):
                            pv = tt % 4
                            pst = ps[pv].bitcast(BF16)[:, 0:128]
                            b.op("tensor", lambda t_, p=p, tt=tt, pst=pst:
                                 t_.transpose(out=pst,
                                              in_=vh[p][:, tt * 128:(tt + 1) * 128],
                                              identity=ident[:]),
                                 waits=[hl, (s_ex, exp_prev),
                                        (s_ve, h * NT + tt - 3)],
                                 inc=s_vt)
                            vt = b.tok(s_vt)
                            b.op("scalar", lambda s, p=p, tt=tt, pst=pst:
                                 s.activation(out=vtm[p][:, tt * 128:(tt + 1) * 128],
                                              in_=pst, func=AF.Identity),
                                 waits=[(s_vt, vt)], inc=s_ve)
                        ve_all = b.tok(s_ve)
                        for half in range(2):
                            q0, q1 = half * 1024, (half + 1) * 1024
                            oe_gate = b.tok(s_oe)
                            of_gate = b.tok(s_of)
                            blocks = []
                            for j in range(16):
                                c0 = max(128 * j, q0)
                                if c0 >= q1:
                                    continue
                                cb = c0
                                while cb < q1:
                                    ce = min(((cb // 512) + 1) * 512, q1)
                                    blocks.append((j, cb, ce))
                                    cb = ce
                            SKEW = 3
                            pend = []
                            seq = list(blocks) + [None] * SKEW
                            for blk_i, blk in enumerate(seq):
                                if blk is not None:
                                    (j, cb, ce) = blk
                                    pp2 = nblk % 4
                                    diag = (cb <= 128 * j < ce)
                                    b.op("tensor", lambda t_, j=j, cb=cb, ce=ce,
                                         pp2=pp2, p=p: t_.matmul(
                                             out=ps[pp2][:, 0:ce - cb],
                                             lhsT=kh[p][:, j * 128:(j + 1) * 128],
                                             rhs=qh[p][:, cb:ce],
                                             start=True, stop=True),
                                         waits=[hl, (s_ve, ve_all),
                                                (s_ex, nblk - 3)],
                                         inc=s_stm)
                                    smt = b.tok(s_stm)
                                    if diag:
                                        off = 128 * j - cb
                                        b.op("vector", lambda v, pp2=pp2, off=off:
                                             v.tensor_tensor(
                                                 out=ps[pp2][:, off:off + 128],
                                                 in0=ps[pp2][:, off:off + 128],
                                                 in1=maskT[:], op=ALU.add),
                                             waits=[(s_stm, smt)], inc=s_msk)
                                        mw = [(s_msk, b.tok(s_msk))]
                                    else:
                                        mw = [(s_stm, smt)]
                                    pa = nblk % 4
                                    b.op("scalar", lambda s, pp2=pp2, pa=pa, cb=cb,
                                         ce=ce: s.activation(
                                             out=att[pa][:, 0:ce - cb],
                                             in_=ps[pp2][:, 0:ce - cb],
                                             func=AF.Exp, bias=expb_t[:]),
                                         waits=mw + [(s_av, nblk - 3)], inc=s_ex)
                                    pend.append((j, cb, ce, pa, b.tok(s_ex)))
                                    nblk += 1
                                if len(pend) > (SKEW if blk is not None else 0):
                                    (j, cb, ce, pa, ext) = pend.pop(0)

                                    def avmm(t_, j=j, cb=cb, ce=ce, pa=pa, p=p,
                                             q0=q0):
                                        st = (j == 0)
                                        sp = (128 * (j + 1) >= ce)
                                        r = (cb - q0) // 512
                                        o0 = cb - q0 - 512 * r
                                        o1 = ce - q0 - 512 * r
                                        t_.matmul(out=pO[r][:, o0:o1],
                                                  lhsT=vtm[p][:, j * 128:(j + 1) * 128],
                                                  rhs=att[pa][:, 0:ce - cb],
                                                  start=st, stop=sp)
                                        return t_.matmul(out=pl[r][:, o0:o1],
                                                         lhsT=ones128[:],
                                                         rhs=att[pa][:, 0:ce - cb],
                                                         start=st, stop=sp)
                                    b.op("tensor", avmm,
                                         waits=[(s_ex, ext), (s_oe, oe_gate),
                                                (s_of, of_gate)],
                                         inc=s_av)
                            avt = b.tok(s_av)
                            def lnev(s):
                                s.activation(out=lntmp[:, 0:512], in_=pl[0][:],
                                             func=AF.Ln)
                                return s.activation(out=lntmp[:, 512:1024],
                                                    in_=pl[1][:], func=AF.Ln)
                            b.op("scalar", lnev, waits=[(s_av, avt)], inc=s_oe)
                            b.op("scalar", lambda s: s.activation(
                                out=rC[:], in_=lntmp[:], func=AF.Exp, scale=-1.0),
                                waits=[(s_oe, b.tok(s_oe))], inc=s_oe)
                            oet2 = b.tok(s_oe)
                            def omul(v, h=h, q0=q0):
                                v.tensor_tensor(
                                    out=o_all[h][:, q0:q0 + 512], in0=pO[0][:],
                                    in1=rC[:, 0:512], op=ALU.mult)
                                return v.tensor_tensor(
                                    out=o_all[h][:, q0 + 512:q0 + 1024],
                                    in0=pO[1][:], in1=rC[:, 512:1024],
                                    op=ALU.mult)
                            b.op("vector", omul,
                                 waits=[(s_oe, oet2), (s_av, avt)], inc=s_of)
            of_all = b.tok(s_of)

            with ExitStack() as a4s:
                p4 = lambda name, shape, dtp: a4s.enter_context(
                    nc.sbuf_tensor(tag + name, shape, dtp))
                wosb = [p4(f"wos{i}", [128, HN * 256], BF16) for i in range(2)]
                yrb = [p4(f"yrb{i}", [128, 256], F32) for i in range(3)]
                xsr = [p4(f"xsr{i}", [128, 256], F32) for i in range(3)]
                borep = p4("borep", [128, D], BF16)
                b.op("gpsimd", lambda g: g.dma_start(out=borep[:], in_=dt["bo_r"]),
                     waits=[(s_of, of_all)], inc=s_wo + "b", inc_n=16)
                wob = (s_wo + "b", b.tok(s_wo + "b"))
                with (
                    nc.psum_tensor(tag + "pyo0", [128, 512], F32) as pyo0,
                    nc.psum_tensor(tag + "pyo1", [128, 512], F32) as pyo1,
                ):
                    pyo = [pyo0, pyo1]
                    for fb in range(16):
                        p = fb % 2
                        fsl = slice(fb * 256, (fb + 1) * 256)
                        b.op("gpsimd", lambda g, fb=fb, p=p: g.dma_start(
                            out=wosb[p][:].rearrange("p (j f) -> p j f", j=HN),
                            in_=dt["wo_t"][fb].rearrange("(j p) f -> p j f", p=128)),
                            waits=[(s_of, of_all), (s_yo, NT * (fb - 1))],
                            inc=s_wo + str(p), inc_n=16)
                        wot = (s_wo + str(p), b.tok(s_wo + str(p)))
                        for t in range(NT):
                            nzy = fb * NT + t
                            pp = t % 2
                            c3 = nzy % 3
                            tsl = slice(t * 128, (t + 1) * 128)
                            b.op("sync", lambda s, c3=c3, tsl=tsl, fsl=fsl:
                                 s.dma_start(out=xsr[c3][:], in_=src[tsl, fsl]),
                                 waits=[(s_ye, nzy - 2)] +
                                       ([(s_of, of_all)] +
                                        list(ready_fn(None)) if nzy < 3 else []),
                                 inc="Bsr", inc_n=16)
                            srt = b.tok("Bsr")

                            def yomm(t_, p=p, pp=pp, tsl=tsl, fsl=fsl):
                                for k in range(HN):
                                    t_.matmul(out=pyo[pp][:, 0:256],
                                              lhsT=o_all[k][:, tsl],
                                              rhs=wosb[p][:, k * 256:(k + 1) * 256],
                                              start=(k == 0), stop=False)
                                return t_.matmul(out=pyo[pp][:, 0:256],
                                                 lhsT=e0_bf[:],
                                                 rhs=borep[:, fsl], start=False,
                                                 stop=True)
                            b.op("tensor", yomm,
                                 waits=[wot, wob, (s_of, of_all),
                                        (s_ye, nzy - 1)],
                                 inc=s_yo)
                            yot = b.tok(s_yo)
                            b.op("vector", lambda v, pp=pp, c3=c3:
                                 v.tensor_tensor(
                                     out=yrb[c3][:], in0=xsr[c3][:],
                                     in1=pyo[pp][:, 0:256], op=ALU.add),
                                 waits=[(s_yo, yot), ("Bsr", srt),
                                        ("Bwr", 16 * (nzy - 2))],
                                 inc=s_ye)
                            yet = b.tok(s_ye)
                            b.op("scalar", lambda s, tsl=tsl, fsl=fsl, c3=c3:
                                 s.dma_start(out=dst[tsl, fsl], in_=yrb[c3][:]),
                                 waits=[(s_ye, yet)], inc="Bwr", inc_n=16)
        tile_w = {t: b.abs_waits([("Bwr", 16 * (240 + t + 1))])
                  for t in range(NT)}
        full_w = b.abs_waits([("Bwr", 16 * 256)])
        return lambda t: full_w if t is None else tile_w[t]

    rdy0 = lambda t: []
    if N_SUBLAYERS == 1:
        a1w = ffn_phase(dt["x"], dt["out"], "A", rdy0)
        b.barrier()
        b.op("gpsimd", lambda g: None, waits=a1w(None))
    elif N_SUBLAYERS == 2:
        a1w = ffn_phase(dt["x"], dt["X1"], "A", rdy0)
        b.barrier()
        a2w = attn_phase(dt["X1"], dt["X1"], "B", a1w)
        b.barrier()
        with nc.sbuf_tensor("cp", [128, D], F32) as cp:
            for t in range(NT):
                b.op("sync", lambda s, t=t: s.dma_start(
                    out=cp[:], in_=dt["X1"][t * 128:(t + 1) * 128, :]),
                    waits=a2w(None) + [("cpw", 16 * t)], inc="cpr", inc_n=16)
                b.op("sync", lambda s, t=t: s.dma_start(
                    out=dt["out"][t * 128:(t + 1) * 128, :], in_=cp[:]),
                    waits=[("cpr", 16 * (t + 1))], inc="cpw", inc_n=16)
            b.op("sync", lambda s: None, waits=[("cpw", 16 * NT)])
    else:
        a1w = ffn_phase(dt["x"], dt["X1"], "A", rdy0)
        b.barrier()
        a2w = attn_phase(dt["X1"], dt["X1"], "B", a1w)
        b.barrier()
        a3w = ffn_phase(dt["X1"], dt["out"], "C", a2w)
        b.barrier()
        b.op("gpsimd", lambda g: None, waits=a3w(None))

    b.emit()
    es.close()
    return nc


_NC_CACHE = None


def _get_nc():
    global _NC_CACHE
    if _NC_CACHE is None:
        _NC_CACHE = build_program()
    return _NC_CACHE


def stage_inputs(inputs):
    ii = {k: np.asarray(v) for k, v in inputs.items()}
    C = lambda a: np.ascontiguousarray(np.asarray(a, dtype=np.float32))

    wq_t = C(ii["L0_attn_Wq"].T.reshape(D, HN, HD).transpose(1, 0, 2))
    wk_t = C(ii["L0_attn_Wk"].T.reshape(D, HN, HD).transpose(1, 0, 2))
    wv_t = C(ii["L0_attn_Wv"].T.reshape(D, HN, HD).transpose(1, 0, 2))
    wo_t = C(ii["L0_attn_Wo"].T.reshape(D, 16, 256).transpose(1, 0, 2))
    bq = C(ii["L0_attn_bq"].reshape(HN, 128).T)
    bk = C(ii["L0_attn_bk"].reshape(HN, 128).T)
    bv = C(ii["L0_attn_bv"].reshape(HN, 128).T)
    bo_r = C(np.tile(np.asarray(ii["L0_attn_bo"], np.float32).reshape(1, D),
                     (128, 1)))
    w1_t = C(ii["L0_ffn_w1q"].T.reshape(1024, NM, 128).transpose(1, 0, 2))
    w3_t = C(ii["L0_ffn_w3q"].T.reshape(1024, NM, 128).transpose(1, 0, 2))
    w2_t = np.zeros((NK2 * 128, D), np.float32)
    w2_t[:2752] = np.asarray(ii["L0_ffn_w2q"], dtype=np.float32).T
    a1_t = np.zeros((D, 128), np.float32)
    a1_t[:, 0:2] = np.asarray(ii["L0_ffn_a1"], np.float32).T
    a3_t = np.zeros((D, 128), np.float32)
    a3_t[:, 0:2] = np.asarray(ii["L0_ffn_a3"], np.float32).T
    a2f = np.asarray(ii["L0_ffn_a2"], dtype=np.float32).T
    a2_t = np.zeros((88 * 128, 128), np.float32)
    for c in range(4):
        for kk in range(NK2):
            rows = 64 if kk == NK2 - 1 else 128
            r0 = 2752 * c + 128 * kk
            blk = c * NK2 + kk
            a2_t[blk * 128:blk * 128 + rows, 0:2] = a2f[r0:r0 + rows]
    b1_t = np.zeros((128, FF), np.float32)
    b1_t[0:2] = np.asarray(ii["L0_ffn_b1"], np.float32).T
    b3_t = np.zeros((128, FF), np.float32)
    b3_t[0:2] = np.asarray(ii["L0_ffn_b3"], np.float32).T
    b2_t = np.zeros((128, D), np.float32)
    b2_t[0:2] = np.asarray(ii["L0_ffn_b2"], np.float32).T

    return dict(wq_t=wq_t, wk_t=wk_t, wv_t=wv_t, wo_t=wo_t, bq=bq, bk=bk,
                bv=bv, bo_r=bo_r, w1_t=w1_t, w3_t=w3_t, w2_t=w2_t, a1_t=a1_t,
                a3_t=a3_t, a2_t=a2_t, b1_t=b1_t, b3_t=b3_t, b2_t=b2_t)


def kernel(**inputs):
    shared = stage_inputs(inputs)
    x = np.asarray(inputs["L0_attn_x"], dtype=np.float32)
    in_maps = [dict(shared, x=np.ascontiguousarray(x[i])) for i in range(8)]

    nc = _get_nc()
    res = run_bass_kernel_spmd(nc, in_maps, core_ids=list(range(8)))
    return np.stack([np.asarray(res.results[i]["out"]) for i in range(8)], axis=0)


if __name__ == "__main__":
    import reference as R
    ins = {k: np.asarray(v) for k, v in R.setup_inputs(0).items()}
    got = kernel(**ins)
    import jax
    import jax.numpy as jnp
    if N_SUBLAYERS == 3:
        want = np.asarray(R.reference(**ins))
    else:
        x = jnp.asarray(ins["L0_attn_x"])
        ln = (x - x.mean(-1, keepdims=True)) / jnp.sqrt(x.var(-1, keepdims=True) + 1e-5)

        def ffn_ref(xx):
            LORA = 0.5
            def ql(v, wq, A, Bm):
                w = jnp.tile(wq, (1, 4))
                return v @ w.T + (v @ A.T @ Bm.T) * LORA
            g = jax.nn.silu(ql(xx, jnp.asarray(ins["L0_ffn_w1q"]), jnp.asarray(ins["L0_ffn_a1"]), jnp.asarray(ins["L0_ffn_b1"])))
            u = ql(xx, jnp.asarray(ins["L0_ffn_w3q"]), jnp.asarray(ins["L0_ffn_a3"]), jnp.asarray(ins["L0_ffn_b3"]))
            return ql(g * u, jnp.asarray(ins["L0_ffn_w2q"]), jnp.asarray(ins["L0_ffn_a2"]), jnp.asarray(ins["L0_ffn_b2"]))
        x1 = x + 0.5 * ffn_ref(ln)
        if N_SUBLAYERS == 1:
            want = np.asarray(x1)
        else:
            ln1 = (x1 - x1.mean(-1, keepdims=True)) / jnp.sqrt(x1.var(-1, keepdims=True) + 1e-5)
            B_, L_, D_, H_ = 8, 2048, 4096, 32
            q = (ln1 @ jnp.asarray(ins["L0_attn_Wq"]).T + jnp.asarray(ins["L0_attn_bq"])).reshape(B_, L_, H_, 128).transpose(0, 2, 1, 3)
            k = (ln1 @ jnp.asarray(ins["L0_attn_Wk"]).T + jnp.asarray(ins["L0_attn_bk"])).reshape(B_, L_, H_, 128).transpose(0, 2, 1, 3)
            v = (ln1 @ jnp.asarray(ins["L0_attn_Wv"]).T + jnp.asarray(ins["L0_attn_bv"])).reshape(B_, L_, H_, 128).transpose(0, 2, 1, 3)
            sc = jnp.einsum("bhqd,bhkd->bhqk", q, k) / math.sqrt(128)
            causal = jnp.tril(jnp.ones((L_, L_), bool))
            sc = jnp.where(causal, sc, -jnp.inf)
            at = jax.nn.softmax(sc, axis=-1)
            o = jnp.einsum("bhqk,bhkd->bhqd", at, v).transpose(0, 2, 1, 3).reshape(B_, L_, D_)
            want = np.asarray(x1 + o @ jnp.asarray(ins["L0_attn_Wo"]).T + jnp.asarray(ins["L0_attn_bo"]))
    d = got - want
    rv = float((d ** 2).mean() / (want ** 2).mean())
    print("residual variance ratio:", rv)
    print("max abs err:", float(np.abs(d).max()))
